# version 03
# baseline (speedup 1.0000x reference)
import numpy as np
import ml_dtypes
import concourse.bass as bass
import concourse.mybir as mybir
from concourse.bass_utils import run_bass_kernel_spmd

F32 = mybir.dt.float32
BF16 = mybir.dt.bfloat16
I32 = mybir.dt.int32
AF = mybir.ActivationFunctionType
ALU = mybir.AluOpType
AX = mybir.AxisListType

NCORE = 8
D = 2048
NCH = D // 128
TOK = 1024
TT = 512
NTT = TOK // TT
H0 = 5120
NH0 = H0 // 128
I1 = 8192
NI1 = I1 // 128
NH = 16
DL = 128
SEQ = 4096
NEG = -1.0e30

ENGS = ("sync", "scalar", "tensor", "vector", "gpsimd")


class SemT:
    __slots__ = ("h", "n", "name")

    def __init__(self, h, name):
        self.h = h
        self.n = 0
        self.name = name


SERIAL = False


class P:
    def __init__(self, nc):
        self.nc = nc
        self.ops = {e: [] for e in ENGS}
        self.all_sems = []
        self.dsem = {}

    def sem(self, name):
        s = SemT(self.nc.alloc_semaphore(name), name)
        self.all_sems.append(s)
        return s

    def op(self, eng, meth, *args, sem=None, inc=1, **kw):
        import traceback
        _st = traceback.extract_stack()[-3]
        if SERIAL:
            for s in self.all_sems:
                if s.n > 0:
                    self.wait(eng, s, s.n)
            if sem is None:
                if eng not in self.dsem:
                    self.dsem[eng] = self.sem(f"dser_{eng}")
                sem = self.dsem[eng]
                inc = 16 if meth == "dma_start" else 1
        def fn(e, meth=meth, args=args, kw=kw, sem=sem, inc=inc, _st=_st):
            try:
                ins = getattr(e, meth)(*args, **kw)
            except Exception as ex:
                raise RuntimeError(f"op {meth} from {_st.filename}:{_st.lineno}: {ex}") from ex
            if sem is not None:
                ins.then_inc(sem.h, inc)
        self.ops[eng].append(fn)
        if sem is not None:
            sem.n += inc
            return sem.n
        return None

    def dma(self, eng, out, in_, sem, **kw):
        return self.op(eng, "dma_start", sem=sem, inc=16, out=out, in_=in_, **kw)

    def wait(self, eng, sem, val=None):
        if val is None:
            val = sem.n
        if val <= 0:
            return
        assert val <= sem.n, f"wait {sem.name}>={val} but builds to {sem.n}"
        def fn(e, sem=sem, v=val):
            e.wait_ge(sem.h, v)
        self.ops[eng].append(fn)

    def seq(self, eng, sem):
        self.wait(eng, sem, sem.n)

    def finalize(self):
        with self.nc.Block() as block:
            for eng in ENGS:
                deco = getattr(block, eng)
                ops = self.ops[eng]
                def body(e, ops=ops):
                    for f in ops:
                        f(e)
                deco(body)


class Phase:
    def __init__(self, nc):
        self.nc = nc
        self.guards = []

    def sb(self, name, shape, dtype):
        g = self.nc.sbuf_tensor(name, shape, dtype, side="right")
        h = g.__enter__()
        self.guards.append(g)
        return h

    def close(self):
        for g in reversed(self.guards):
            g.__exit__(None, None, None)
        self.guards = []


def make_consts():
    ident = np.eye(128, dtype=np.float32)
    return {"c_ident": ident, "c_identb": ident.astype(ml_dtypes.bfloat16)}


def decl_consts(nc, attn=False, masks=False):
    d = {
        "ident": nc.dram_tensor("c_ident", [128, 128], F32, kind="ExternalInput").ap(),
    }
    if attn:
        d["identb"] = nc.dram_tensor("c_identb", [128, 128], BF16, kind="ExternalInput").ap()
    if masks:
        d["maskQc"] = nc.dram_tensor("c_maskQc", [128, 512], BF16, kind="ExternalInput").ap()
        d["maskTc"] = nc.dram_tensor("c_maskTc", [16, 128, 512], BF16, kind="ExternalInput").ap()
    return d


class Common:
    def __init__(self, p, nc, consts_aps):
        self.p = p
        self.nc = nc
        self.X = nc.alloc_sbuf_tensor("X", [128, NCH * TOK], F32)
        self.XLN = nc.alloc_sbuf_tensor("XLN", [128, NCH * TT], BF16)
        self.a_bc = nc.alloc_sbuf_tensor("a_bc", [128, TT], F32)
        self.b_bc = nc.alloc_sbuf_tensor("b_bc", [128, TT], F32)
        self.row = nc.alloc_sbuf_tensor("row", [1, 2048], F32)
        self.mrow = nc.alloc_sbuf_tensor("mrow", [1, TOK], BF16)
        self.stat = nc.alloc_sbuf_tensor("stat", [1, 32], F32)
        self.cols = nc.alloc_sbuf_tensor("cols", [128, 16], F32)
        self.scr = nc.alloc_sbuf_tensor("scr", [128, 2 * TT], F32)
        self.sgsu = nc.alloc_sbuf_tensor("sgsu", [128, 2 * TT], BF16)
        self.srow = nc.alloc_sbuf_tensor("srow", [1, 512], F32)
        self.ones_c = nc.alloc_sbuf_tensor("ones_c", [128, 1], F32)
        self.ones_r = nc.alloc_sbuf_tensor("ones_r", [1, 128], F32)
        self.negz_rb = nc.alloc_sbuf_tensor("negz_rb", [1, 512], BF16)
        self.ones_rb = nc.alloc_sbuf_tensor("ones_rb", [1, 128], BF16)
        self.ones_cb = nc.alloc_sbuf_tensor("ones_cb", [128, 1], BF16)
        self.ident = nc.alloc_sbuf_tensor("ident", [128, 128], F32)
        self.kc = nc.alloc_sbuf_tensor("kc", [128, 4], F32)
        self.ps = nc.alloc_psum_tensor("ps", [128, 8 * 512], F32)
        self.qG = p.sem("qG")
        self.qS = p.sem("qS")
        self.qY = p.sem("qY")
        self.cT = p.sem("cT")
        self.cV = p.sem("cV")
        self.cS = p.sem("cS")
        self.cP = p.sem("cP")
        self.cc = p.sem("cc")
        self.consts_aps = consts_aps
        p.dma("gpsimd", self.ident[:], consts_aps["ident"], self.qG)
        p.op("gpsimd", "memset", self.ones_c[:], 1.0, sem=self.cP)
        p.op("gpsimd", "memset", self.ones_r[:], 1.0, sem=self.cP)
        p.op("gpsimd", "memset", self.ones_cb[:], 1.0, sem=self.cP)
        p.op("gpsimd", "memset", self.negz_rb[:], -128.0, sem=self.cP)
        p.op("gpsimd", "memset", self.kc[:, 0:1], 1e-5, sem=self.cP)
        p.op("gpsimd", "memset", self.kc[:, 1:2], 0.08838834764831845, sem=self.cP)
        p.op("gpsimd", "memset", self.kc[:, 2:3], 0.01, sem=self.cP)
        p.op("gpsimd", "memset", self.kc[:, 3:4], 1.0, sem=self.cP)
        p.op("gpsimd", "memset", self.ones_rb[:], 1.0, sem=self.cP)
        self.initq, self.initp = self.qG.n, self.cP.n

    def wait_init(self):
        for eng in ("tensor", "scalar", "vector", "sync"):
            self.p.wait(eng, self.cP, self.initp)
            self.p.wait(eng, self.qG, self.initq)

    def bank(self, b, width=512, part=128):
        return self.ps[0:part, b * 512: b * 512 + width]

    def xc(self, c, tt=None):
        if tt is None:
            return self.X[:, c * TOK:(c + 1) * TOK]
        return self.X[:, c * TOK + tt * TT: c * TOK + (tt + 1) * TT]

    def xln(self, c):
        return self.XLN[:, c * TT:(c + 1) * TT]


def emit_ln_tt(cm, tt, xwait=()):
    p = cm.p
    for eng in ("tensor", "scalar", "gpsimd"):
        for s, v in xwait:
            p.wait(eng, s, v)
    p.wait("scalar", cm.cV, cm.cV.n)
    p.wait("scalar", cm.cP, cm.cP.n)
    rsum = cm.row[0:1, 0:512]
    rstd = cm.row[0:1, 512:1024]
    nmr = cm.row[0:1, 1024:1536]
    rsq = cm.row[0:1, 1536:2048]
    for c in range(NCH):
        p.op("tensor", "matmul", cm.bank(0, part=1), cm.ones_c[:], cm.xc(c, tt),
             start=(c == 0), stop=(c == NCH - 1), sem=cm.cT)
    sq_used = {}
    sq_v = []
    for c in range(NCH + 1):
        if c < NCH:
            par = c % 2
            sq = cm.scr[:, par * TT:(par + 1) * TT]
            if c >= 2:
                p.wait("scalar", cm.cT, sq_used[par])
            sq_v.append(p.op("scalar", "activation", sq, cm.xc(c, tt), AF.Square,
                             sem=cm.cS))
        else:
            sq_v.append(p.op("scalar", "activation", cm.stat[0:1, 30:31],
                             cm.stat[0:1, 29:30], AF.Square, sem=cm.cS))
        if c >= 1:
            cc = c - 1
            parc = cc % 2
            sqc = cm.scr[:, parc * TT:(parc + 1) * TT]
            p.wait("tensor", cm.cS, sq_v[c])
            sq_used[parc] = p.op("tensor", "matmul", cm.bank(1, part=1),
                                 cm.ones_c[:], sqc, start=(cc == 0),
                                 stop=(cc == NCH - 1), sem=cm.cT)
    p.wait("vector", cm.cT, cm.cT.n)
    p.op("vector", "tensor_scalar_mul", rsum, cm.bank(0, part=1), 1.0 / D, sem=cm.cV)
    p.op("vector", "tensor_scalar_mul", rsq, cm.bank(1, part=1), 1.0 / D, sem=cm.cV)
    p.seq("vector", cm.cV)
    p.op("vector", "tensor_mul", nmr, rsum, rsum, sem=cm.cV)
    p.seq("vector", cm.cV)
    vv = p.op("vector", "tensor_sub", rstd, rsq, nmr, sem=cm.cV)
    p.wait("scalar", cm.cV, vv)
    vs = p.op("scalar", "activation", rsq, rstd, AF.Sqrt, bias=cm.kc[0:1, 0:1],
              sem=cm.cS)
    p.wait("vector", cm.cS, vs)
    p.op("vector", "reciprocal", rstd, rsq, sem=cm.cV)
    p.seq("vector", cm.cV)
    p.op("vector", "tensor_scalar_mul", nmr, rsum, -1.0, sem=cm.cV)
    p.seq("vector", cm.cV)
    vv = p.op("vector", "tensor_mul", rsum, nmr, rstd, sem=cm.cV)
    p.wait("tensor", cm.cV, vv)
    p.op("tensor", "matmul", cm.bank(2), cm.ones_r[:], rstd, start=True, stop=True,
         sem=cm.cT)
    vt = p.op("tensor", "matmul", cm.bank(3), cm.ones_r[:], rsum, start=True,
              stop=True, sem=cm.cT)
    p.wait("vector", cm.cT, vt)
    p.op("vector", "tensor_copy", cm.a_bc[:], cm.bank(2), sem=cm.cV)
    vv = p.op("vector", "tensor_copy", cm.b_bc[:], cm.bank(3), sem=cm.cV)
    p.wait("gpsimd", cm.cV, vv)
    last_add = {}
    for c in range(NCH):
        par = c % 2
        dst = cm.scr[:, par * TT:(par + 1) * TT]
        if c >= 2:
            p.wait("gpsimd", cm.cV, last_add[par])
        vp = p.op("gpsimd", "tensor_tensor", dst, cm.xc(c, tt), cm.a_bc[:], ALU.mult,
                  sem=cm.cP)
        p.wait("vector", cm.cP, vp)
        last_add[par] = p.op("vector", "tensor_tensor", cm.xln(c), dst, cm.b_bc[:],
                             ALU.add, sem=cm.cV)
    return cm.cV.n


def emit_tern_prep(cm, w_fc, w_proj, sg_fc, sg_pj, wt_fc, wt_pj):
    p, nc = cm.p, cm.nc
    ph = Phase(nc)
    wbuf = ph.sb("tern_wbuf", [128, 2 * 2048], F32)
    sa = ph.sb("tern_sa", [128, 2048], BF16)
    sb = ph.sb("tern_sb", [128, 2048], BF16)
    s2 = ph.sb("tern_s2", [128, 2 * 2048], BF16)
    part = ph.sb("tern_part", [128, 128], F32)
    specs = [("fc", w_fc, 5120, 2048, sg_fc, wt_fc, 0),
             ("pj", w_proj, 2048, 5120, sg_pj, wt_pj, 1)]
    pcol = 0
    pbase = {}
    red_done = {}
    for (nm, w, R, C, sg, wt, widx) in specs:
        pbase[nm] = pcol
        for r in range(R // 128):
            for j in range((C + 2047) // 2048):
                cw = min(2048, C - j * 2048)
                b = pcol % 2
                buf = wbuf[:, b * 2048: b * 2048 + cw]
                if ("prev", b) in red_done:
                    p.wait("gpsimd", cm.cV, red_done[("prev", b)])
                v = p.dma("gpsimd", buf, w[r * 128:(r + 1) * 128, j * 2048:j * 2048 + cw],
                          cm.qG)
                p.wait("vector", cm.qG, v)
                red_done[("prev", b)] = p.op(
                    "vector", "tensor_reduce", part[:, pcol:pcol + 1], buf, AX.X,
                    ALU.add, apply_absolute_value=True, sem=cm.cV)
                pcol += 1
        p.seq("vector", cm.cV)
        p.op("vector", "tensor_reduce", part[:, 120 + widx:121 + widx],
             part[:, pbase[nm]:pcol], AX.X, ALU.add, sem=cm.cV)
    p.wait("tensor", cm.cV, cm.cV.n)
    p.op("tensor", "matmul", cm.bank(0, width=1, part=1), part[:, 120:121],
         cm.ones_c[:], start=True, stop=True, sem=cm.cT)
    vt = p.op("tensor", "matmul", cm.bank(1, width=1, part=1), part[:, 121:122],
              cm.ones_c[:], start=True, stop=True, sem=cm.cT)
    p.wait("vector", cm.cT, vt)
    st = cm.stat
    p.op("vector", "tensor_copy", st[0:1, 0:1], cm.bank(0, width=1, part=1), sem=cm.cV)
    p.op("vector", "tensor_copy", st[0:1, 1:2], cm.bank(1, width=1, part=1), sem=cm.cV)
    p.seq("vector", cm.cV)
    p.op("vector", "tensor_scalar", st[0:1, 2:3], st[0:1, 0:1], 1.0 / (5120 * 2048),
         1e-5, ALU.mult, ALU.max, sem=cm.cV)
    p.op("vector", "tensor_scalar", st[0:1, 3:4], st[0:1, 1:2], 1.0 / (5120 * 2048),
         1e-5, ALU.mult, ALU.max, sem=cm.cV)
    p.seq("vector", cm.cV)
    p.op("vector", "tensor_scalar_mul", st[0:1, 4:5], st[0:1, 2:3], 0.7, sem=cm.cV)
    p.op("vector", "tensor_scalar_mul", st[0:1, 5:6], st[0:1, 2:3], -0.7, sem=cm.cV)
    p.op("vector", "tensor_scalar_mul", st[0:1, 6:7], st[0:1, 3:4], 0.7, sem=cm.cV)
    p.op("vector", "tensor_scalar_mul", st[0:1, 7:8], st[0:1, 3:4], -0.7, sem=cm.cV)
    p.op("vector", "tensor_mul", st[0:1, 8:9], st[0:1, 2:3], st[0:1, 2:3], sem=cm.cV)
    p.seq("vector", cm.cV)
    p.op("vector", "tensor_mul", st[0:1, 9:10], st[0:1, 8:9], st[0:1, 3:4], sem=cm.cV)
    p.seq("vector", cm.cV)
    vv = p.op("vector", "tensor_scalar_mul", st[0:1, 10:11], st[0:1, 9:10], 0.0625,
              sem=cm.cV)
    p.wait("tensor", cm.cV, vv)
    for (si, di) in [(4, 0), (5, 1), (6, 2), (7, 3), (10, 4)]:
        p.op("tensor", "matmul", cm.bank(2, width=1), cm.ones_r[:], st[0:1, si:si + 1],
             start=True, stop=True, sem=cm.cT)
        p.wait("vector", cm.cT, cm.cT.n)
        vv = p.op("vector", "tensor_copy", cm.cols[:, di:di + 1], cm.bank(2, width=1),
                  sem=cm.cV)
        p.wait("tensor", cm.cV, vv)
    stats_done = cm.cV.n
    p.wait("scalar", cm.cV, stats_done)
    p.wait("gpsimd", cm.cV, stats_done)
    store_done = {}
    s2_done = {}
    ci = 0
    for (nm, w, R, C, sg, wt, widx) in specs:
        thr = cm.cols[:, 2 * widx:2 * widx + 1]
        nthr = cm.cols[:, 2 * widx + 1:2 * widx + 2]
        for r in range(R // 128):
            for j in range((C + 2047) // 2048):
                cw = min(2048, C - j * 2048)
                t = ci % 2
                ci += 1
                buf = wbuf[:, t * 2048: t * 2048 + cw]
                s2b = s2[:, t * 2048: t * 2048 + cw]
                if ("sb", t) in s2_done:
                    p.wait("gpsimd", cm.cS, s2_done[("sb", t)])
                v = p.dma("gpsimd", buf, w[r * 128:(r + 1) * 128, j * 2048:j * 2048 + cw],
                          cm.qG)
                p.wait("scalar", cm.qG, v)
                if "va" in s2_done:
                    p.wait("scalar", cm.cV, s2_done["va"])
                p.op("scalar", "activation", sa[:, 0:cw], buf, AF.Sign, bias=nthr,
                     sem=cm.cS)
                vs = p.op("scalar", "activation", sb[:, 0:cw], buf, AF.Sign, bias=thr,
                          sem=cm.cS)
                s2_done[("sb", t)] = vs
                p.wait("vector", cm.cS, vs)
                if ("st", t) in store_done:
                    p.wait("vector", cm.qS, store_done[("st", t)])
                va = p.op("vector", "tensor_tensor", s2b, sa[:, 0:cw], sb[:, 0:cw],
                          ALU.add, sem=cm.cV)
                s2_done["va"] = va
                p.wait("scalar", cm.cV, va)
                store_done[("st", t)] = p.dma(
                    "scalar", sg[r * 128:(r + 1) * 128, j * 2048:j * 2048 + cw], s2b,
                    cm.qS)
    ph.close()
    cm.dbg_pass2_done = cm.qS.n
    p.wait("sync", cm.qS, cm.qS.n)
    ph2 = Phase(nc)
    tbuf = ph2.sb("tern_tbuf", [128, 2 * 5120], BF16)
    tdone = {}
    for (nm, w, R, C, sg, wt, widx) in specs:
        for j in range(C // 128):
            t = j % 2
            tb = tbuf[:, t * 5120: t * 5120 + R]
            if ("tp", t) in tdone:
                p.wait("sync", cm.qS, tdone[("tp", t)])
            v = p.dma("sync", tb, sg[0:R, j * 128:(j + 1) * 128], cm.qY, transpose=True)
            p.wait("sync", cm.qY, v)
            tdone[("tp", t)] = p.dma("sync", wt[j * 128:(j + 1) * 128, 0:R], tb,
                                     cm.qS)
    ph2.close()
    p.wait("gpsimd", cm.qS, cm.qS.n)
    return cm.qS.n


def emit_cast_weights(cm, raws, scratches):
    p = cm.p
    return [p.dma("gpsimd", sc, raw, cm.qG) for raw, sc in zip(raws, scratches)]


def emit_transpose_weights(cm, items, tallR):
    p, nc = cm.p, cm.nc
    ph = Phase(nc)
    tbuf = ph.sb("w_tbuf", [128, tallR], BF16)
    for (_, _, cv) in items:
        p.wait("sync", cm.qG, cv)
    for (sc, wt, cv) in items:
        R, C = sc.shape
        for j in range(C // 128):
            tb = tbuf[:, 0:R]
            v = p.dma("sync", tb, sc[0:R, j * 128:(j + 1) * 128], cm.qY,
                      transpose=True)
            p.wait("sync", cm.qY, v)
            vs = p.dma("sync", wt[j * 128:(j + 1) * 128, 0:R], tb, cm.qS)
            p.wait("sync", cm.qS, vs)
    ready = cm.qS.n
    ph.close()
    return ready


def emit_scale_cols(cm, scale_ap_1d, n, dst, fold=None):
    p = cm.p
    row2 = scale_ap_1d.rearrange("(a b) -> a b", a=1)
    for seg in range((n * 128 + 511) // 512):
        lo = seg * 512
        hi = min(n * 128, lo + 512)
        p.wait("gpsimd", cm.cT, cm.cT.n)
        vq = p.dma("gpsimd", cm.srow[0:1, 0:hi - lo], row2[0:1, lo:hi], cm.qG)
        p.wait("tensor", cm.qG, vq)
        for c in range(lo // 128, hi // 128):
            p.op("tensor", "matmul", cm.bank(7, width=1),
                 cm.srow[0:1, c * 128 - lo:(c + 1) * 128 - lo], cm.ones_c[0:1, 0:1],
                 start=True, stop=True, sem=cm.cT)
            p.wait("vector", cm.cT, cm.cT.n)
            if fold is None:
                vv = p.op("vector", "tensor_copy", dst[:, c:c + 1],
                          cm.bank(7, width=1), sem=cm.cV)
            else:
                vv = p.op("vector", "tensor_scalar_mul", dst[:, c:c + 1],
                          cm.bank(7, width=1), fold, sem=cm.cV)
            p.wait("tensor", cm.cV, vv)
    return cm.cV.n


def emit_ffn_l0_tt(cm, tt, wt_fc, wt_pj, wt_ready, xln_ready, hbuf, fslab, pslab):
    p = cm.p
    p.wait("gpsimd", cm.qS, wt_ready)
    p.wait("tensor", cm.cV, xln_ready)
    p.wait("scalar", cm.cV, xln_ready)
    guard = {}
    for hc in range(NH0):
        t = hc % 2
        sl = fslab[:, t * NCH * 128:(t + 1) * NCH * 128]
        if hc >= 2:
            p.wait("gpsimd", cm.cT, cm.cT.n)
        vq = p.dma("gpsimd", sl,
                   wt_fc.rearrange("(k p) h -> p k h", p=128)[:, :, hc * 128:(hc + 1) * 128],
                   cm.qG)
        bnk = hc % 2
        p.wait("tensor", cm.qG, vq)
        for k in range(NCH):
            vt = p.op("tensor", "matmul", cm.bank(bnk), sl[:, k * 128:(k + 1) * 128],
                      cm.xln(k), start=(k == 0), stop=(k == NCH - 1), sem=cm.cT)
        p.wait("scalar", cm.cT, vt)
        par = hc % 2
        rb = cm.sgsu[:, par * TT:(par + 1) * TT]
        if ("h", par) in guard:
            p.wait("scalar", cm.cV, guard[("h", par)])
        vs = p.op("scalar", "activation", rb, cm.bank(bnk), AF.Relu, sem=cm.cS)
        p.wait("vector", cm.cS, vs)
        guard[("h", par)] = p.op("vector", "tensor_mul", hbuf[:, hc * TT:(hc + 1) * TT],
                                 rb, rb, sem=cm.cV)
        p.wait("tensor", cm.cS, vs)
    p.wait("vector", cm.cV, cm.cV.n)
    for dc in range(NCH):
        t = dc % 2
        sl = pslab[:, t * NH0 * 128:(t + 1) * NH0 * 128]
        if dc >= 2:
            p.wait("gpsimd", cm.cT, cm.cT.n)
        vq = p.dma("gpsimd", sl,
                   wt_pj.rearrange("(k p) h -> p k h", p=128)[:, :, dc * 128:(dc + 1) * 128],
                   cm.qG)
        bnk = 2 + dc % 2
        p.wait("tensor", cm.qG, vq)
        p.wait("tensor", cm.cV, cm.cV.n)
        for k in range(NH0):
            vt = p.op("tensor", "matmul", cm.bank(bnk), sl[:, k * 128:(k + 1) * 128],
                      hbuf[:, k * TT:(k + 1) * TT], start=(k == 0), stop=(k == NH0 - 1),
                      sem=cm.cT)
        p.wait("vector", cm.cT, vt)
        vv = p.op("vector", "scalar_tensor_tensor", cm.xc(dc, tt), cm.bank(bnk),
                  cm.cols[:, 4:5], cm.xc(dc, tt), ALU.mult, ALU.add, sem=cm.cV)
        p.wait("tensor", cm.cV, vv)


def emit_ffn_l1_tt(cm, tt, wt_g, wt_u, wt_d, gsc, usc, dsc, pre_ready, xln_ready,
                   hbuf, gslab, uslab, dslab):
    p = cm.p
    p.wait("gpsimd", cm.qS, pre_ready)
    p.wait("tensor", cm.cV, xln_ready)
    p.wait("scalar", cm.cV, xln_ready)
    guard = {}
    for ic in range(NI1):
        t = ic % 2
        gsl = gslab[:, t * NCH * 128:(t + 1) * NCH * 128]
        usl = uslab[:, t * NCH * 128:(t + 1) * NCH * 128]
        if ic >= 2:
            p.wait("gpsimd", cm.cT, cm.cT.n)
        p.dma("gpsimd", gsl,
              wt_g.rearrange("(k p) h -> p k h", p=128)[:, :, ic * 128:(ic + 1) * 128],
              cm.qG)
        vq = p.dma("gpsimd", usl,
                   wt_u.rearrange("(k p) h -> p k h", p=128)[:, :, ic * 128:(ic + 1) * 128],
                   cm.qG)
        p.wait("tensor", cm.qG, vq)
        bg = ic % 2
        bu = 2 + ic % 2
        for k in range(NCH):
            vtg = p.op("tensor", "matmul", cm.bank(bg), gsl[:, k * 128:(k + 1) * 128],
                       cm.xln(k), start=(k == 0), stop=(k == NCH - 1), sem=cm.cT)
        for k in range(NCH):
            vtu = p.op("tensor", "matmul", cm.bank(bu), usl[:, k * 128:(k + 1) * 128],
                       cm.xln(k), start=(k == 0), stop=(k == NCH - 1), sem=cm.cT)
        par = ic % 2
        sg = cm.sgsu[:, par * TT:(par + 1) * TT]
        su = cm.scr[:, par * TT:(par + 1) * TT]
        if ("h", par) in guard:
            p.wait("scalar", cm.cV, guard[("h", par)])
        p.wait("scalar", cm.cT, vtg)
        p.op("scalar", "activation", sg, cm.bank(bg), AF.Silu,
             scale=gsc[:, ic:ic + 1], sem=cm.cS)
        p.wait("scalar", cm.cT, vtu)
        vs = p.op("scalar", "activation", su, cm.bank(bu), AF.Copy,
                  scale=usc[:, ic:ic + 1], sem=cm.cS)
        p.wait("vector", cm.cS, vs)
        guard[("h", par)] = p.op("vector", "tensor_mul", hbuf[:, ic * TT:(ic + 1) * TT],
                                 sg, su, sem=cm.cV)
        p.wait("tensor", cm.cS, vs)
    p.wait("tensor", cm.cV, cm.cV.n)
    for ic in range(NI1):
        vt = p.op("tensor", "matmul", cm.bank(4, part=1), cm.ones_cb[:],
                  hbuf[:, ic * TT:(ic + 1) * TT], start=(ic == 0),
                  stop=(ic == NI1 - 1), sem=cm.cT)
    p.wait("vector", cm.cT, vt)
    vv = p.op("vector", "tensor_copy", cm.mrow[0:1, 0:512], cm.bank(4, part=1),
              sem=cm.cV)
    p.wait("tensor", cm.cV, vv)
    NIQ = 16
    half_used = {}
    for dc in range(NCH):
        bnk = 5 + dc % 2
        for qt in range(NI1 // NIQ):
            hf = qt % 2
            dsl = dslab[:, hf * NIQ * 128:(hf + 1) * NIQ * 128]
            if hf in half_used:
                p.wait("gpsimd", cm.cT, half_used[hf])
            vq = p.dma("gpsimd", dsl,
                       wt_d.rearrange("(k p) h -> p k h", p=128)[:, qt * NIQ:(qt + 1) * NIQ, dc * 128:(dc + 1) * 128],
                       cm.qG)
            p.wait("tensor", cm.qG, vq)
            for k in range(NIQ):
                gk = qt * NIQ + k
                vt = p.op("tensor", "matmul", cm.bank(bnk),
                          dsl[:, k * 128:(k + 1) * 128],
                          hbuf[:, gk * TT:(gk + 1) * TT],
                          start=(gk == 0), stop=False, sem=cm.cT)
            half_used[hf] = vt
        vt = p.op("tensor", "matmul", cm.bank(bnk), cm.negz_rb[0:1, 0:128], cm.mrow[0:1, 0:512],
                  start=False, stop=True, sem=cm.cT)
        p.wait("vector", cm.cT, vt)
        vv = p.op("vector", "scalar_tensor_tensor", cm.xc(dc, tt), cm.bank(bnk),
                  dsc[:, dc:dc + 1], cm.xc(dc, tt), ALU.mult, ALU.add, sem=cm.cV)
        p.wait("tensor", cm.cV, vv)


def sync_output_store(cm, out):
    p = cm.p
    p.wait("sync", cm.cV, cm.cV.n)
    p.wait("sync", cm.cP, cm.cP.n)
    vo = p.dma("sync", out.rearrange("(c p) t -> p c t", p=128),
               cm.X[:].rearrange("p (c t) -> p c t", c=NCH), cm.qY)
    p.wait("sync", cm.qY, vo)


def build_l0ffn_launch(debug=False):
    nc = bass.Bass("TRN2", target_bir_lowering=False,
                   dynamic_dma_scratch_size=16384)
    xT = nc.dram_tensor("xT", [D, TOK], F32, kind="ExternalInput").ap()
    w_fc = nc.dram_tensor("L0_ffn_w_fc", [H0, D], F32, kind="ExternalInput").ap()
    w_pj = nc.dram_tensor("L0_ffn_w_proj", [D, H0], F32, kind="ExternalInput").ap()
    consts = decl_consts(nc)
    out = nc.dram_tensor("xT_out", [D, TOK], F32, kind="ExternalOutput").ap()
    sg_fc = nc.dram_tensor("sg_fc", [H0, D], BF16).ap()
    sg_pj = nc.dram_tensor("sg_pj", [D, H0], BF16).ap()
    wt_fc = nc.dram_tensor("wt_fc", [D, H0], BF16).ap()
    wt_pj = nc.dram_tensor("wt_pj", [H0, D], BF16).ap()

    p = P(nc)
    cm = Common(p, nc, consts)
    cm.wait_init()
    vx = p.dma("sync", cm.X[:].rearrange("p (c t) -> p c t", c=NCH),
               xT.rearrange("(c p) t -> p c t", p=128), cm.qY)
    wt_ready = emit_tern_prep(cm, w_fc, w_pj, sg_fc, sg_pj, wt_fc, wt_pj)
    ph = Phase(nc)
    hbuf = ph.sb("l0_h", [128, NH0 * TT], BF16)
    fslab = ph.sb("l0_fs", [128, 2 * NCH * 128], BF16)
    pslab = ph.sb("l0_ps", [128, 2 * NH0 * 128], BF16)
    for tt in range(NTT):
        xln_ready = emit_ln_tt(cm, tt, xwait=((cm.qY, vx),))
        emit_ffn_l0_tt(cm, tt, wt_fc, wt_pj, wt_ready, xln_ready, hbuf, fslab, pslab)
    ph.close()
    sync_output_store(cm, out)
    p.finalize()
    return nc


def build_l1ffn_launch(debug=False):
    nc = bass.Bass("TRN2", target_bir_lowering=False,
                   dynamic_dma_scratch_size=16384)
    xT = nc.dram_tensor("xT", [D, TOK], F32, kind="ExternalInput").ap()
    gw = nc.dram_tensor("L1_ffn_gate_w", [I1, D], I32, kind="ExternalInput").ap()
    gs = nc.dram_tensor("L1_ffn_gate_scale", [I1], F32, kind="ExternalInput").ap()
    uw = nc.dram_tensor("L1_ffn_up_w", [I1, D], I32, kind="ExternalInput").ap()
    us = nc.dram_tensor("L1_ffn_up_scale", [I1], F32, kind="ExternalInput").ap()
    dw = nc.dram_tensor("L1_ffn_down_w", [D, I1], I32, kind="ExternalInput").ap()
    ds = nc.dram_tensor("L1_ffn_down_scale", [D], F32, kind="ExternalInput").ap()
    consts = decl_consts(nc)
    out = nc.dram_tensor("xT_out", [D, TOK], F32, kind="ExternalOutput").ap()
    sc_g = nc.dram_tensor("sc_g", [I1, D], BF16).ap()
    sc_u = nc.dram_tensor("sc_u", [I1, D], BF16).ap()
    sc_d = nc.dram_tensor("sc_d", [D, I1], BF16).ap()
    wt_g = nc.dram_tensor("wt_g", [D, I1], BF16).ap()
    wt_u = nc.dram_tensor("wt_u", [D, I1], BF16).ap()
    wt_d = nc.dram_tensor("wt_d", [I1, D], BF16).ap()

    p = P(nc)
    cm = Common(p, nc, consts)
    cm.wait_init()
    vx = p.dma("sync", cm.X[:].rearrange("p (c t) -> p c t", c=NCH),
               xT.rearrange("(c p) t -> p c t", p=128), cm.qY)
    cvs = emit_cast_weights(cm, [gw, uw, dw], [sc_g, sc_u, sc_d])
    wt_ready = emit_transpose_weights(
        cm, list(zip([sc_g, sc_u, sc_d], [wt_g, wt_u, wt_d], cvs)), tallR=I1)
    gsc = nc.alloc_sbuf_tensor("gsc", [128, NI1], F32)
    usc = nc.alloc_sbuf_tensor("usc", [128, NI1], F32)
    dsc = nc.alloc_sbuf_tensor("dsc", [128, NCH], F32)
    emit_scale_cols(cm, gs, NI1, gsc)
    emit_scale_cols(cm, us, NI1, usc)
    sc_ready = emit_scale_cols(cm, ds, NCH, dsc, fold=0.5)
    ph = Phase(nc)
    hbuf = ph.sb("l1_h", [128, NI1 * TT], BF16)
    for tt in range(NTT):
        xln_ready = emit_ln_tt(cm, tt, xwait=((cm.qY, vx), (cm.cV, sc_ready)))
        ph2 = Phase(nc)
        gslab = ph2.sb(f"l1_gs{tt}", [128, 2 * NCH * 128], BF16)
        uslab = ph2.sb(f"l1_us{tt}", [128, 2 * NCH * 128], BF16)
        dslab = ph2.sb(f"l1_ds{tt}", [128, 2 * 16 * 128], BF16)
        emit_ffn_l1_tt(cm, tt, wt_g, wt_u, wt_d, gsc, usc, dsc, wt_ready, xln_ready,
                       hbuf, gslab, uslab, dslab)
        ph2.close()
    ph.close()
    sync_output_store(cm, out)
    p.finalize()
    return nc


def kernel(**inputs):
    raise NotImplementedError("under construction")


ISQ = 0.08838834764831845


def emit_brow(cm, qb_1d, dst_brow, scale):
    p = cm.p
    row2 = qb_1d.rearrange("(a b) -> a b", a=1)
    for seg in range(4):
        lo, hi = seg * 512, (seg + 1) * 512
        p.wait("gpsimd", cm.cV, cm.cV.n)
        vq = p.dma("gpsimd", cm.srow[0:1, :], row2[0:1, lo:hi], cm.qG)
        p.wait("vector", cm.qG, vq)
        p.op("vector", "tensor_scalar_mul", dst_brow[0:1, lo:hi], cm.srow[0:1, :],
             scale, sem=cm.cV)
    return cm.cV.n


def emit_ln_src_tt(cm, srcfn, xwait=(), src_bf16=False):
    p = cm.p
    ones = cm.ones_cb if src_bf16 else cm.ones_c
    for eng in ("tensor", "scalar", "gpsimd"):
        for s, v in xwait:
            p.wait(eng, s, v)
    p.wait("scalar", cm.cV, cm.cV.n)
    p.wait("scalar", cm.cP, cm.cP.n)
    rsum = cm.row[0:1, 0:512]
    rstd = cm.row[0:1, 512:1024]
    nmr = cm.row[0:1, 1024:1536]
    rsq = cm.row[0:1, 1536:2048]
    for c in range(NCH):
        p.op("tensor", "matmul", cm.bank(0, part=1), ones[:], srcfn(c),
             start=(c == 0), stop=(c == NCH - 1), sem=cm.cT)
    sq_used = {}
    sq_v = []
    for c in range(NCH + 1):
        if c < NCH:
            par = c % 2
            sq = cm.scr[:, par * TT:(par + 1) * TT]
            if c >= 2:
                p.wait("scalar", cm.cT, sq_used[par])
            sq_v.append(p.op("scalar", "activation", sq, srcfn(c), AF.Square,
                             sem=cm.cS))
        else:
            sq_v.append(p.op("scalar", "activation", cm.stat[0:1, 30:31],
                             cm.stat[0:1, 29:30], AF.Square, sem=cm.cS))
        if c >= 1:
            cc = c - 1
            parc = cc % 2
            sqc = cm.scr[:, parc * TT:(parc + 1) * TT]
            p.wait("tensor", cm.cS, sq_v[c])
            sq_used[parc] = p.op("tensor", "matmul", cm.bank(1, part=1),
                                 cm.ones_c[:], sqc, start=(cc == 0),
                                 stop=(cc == NCH - 1), sem=cm.cT)
    p.wait("vector", cm.cT, cm.cT.n)
    p.op("vector", "tensor_scalar_mul", rsum, cm.bank(0, part=1), 1.0 / D, sem=cm.cV)
    p.op("vector", "tensor_scalar_mul", rsq, cm.bank(1, part=1), 1.0 / D, sem=cm.cV)
    p.seq("vector", cm.cV)
    p.op("vector", "tensor_mul", nmr, rsum, rsum, sem=cm.cV)
    p.seq("vector", cm.cV)
    vv = p.op("vector", "tensor_sub", rstd, rsq, nmr, sem=cm.cV)
    p.wait("scalar", cm.cV, vv)
    vs = p.op("scalar", "activation", rsq, rstd, AF.Sqrt, bias=cm.kc[0:1, 0:1],
              sem=cm.cS)
    p.wait("vector", cm.cS, vs)
    p.op("vector", "reciprocal", rstd, rsq, sem=cm.cV)
    p.seq("vector", cm.cV)
    p.op("vector", "tensor_scalar_mul", nmr, rsum, -1.0, sem=cm.cV)
    p.seq("vector", cm.cV)
    vv = p.op("vector", "tensor_mul", rsum, nmr, rstd, sem=cm.cV)
    p.wait("tensor", cm.cV, vv)
    p.op("tensor", "matmul", cm.bank(2), cm.ones_r[:], rstd, start=True, stop=True,
         sem=cm.cT)
    vt = p.op("tensor", "matmul", cm.bank(3), cm.ones_r[:], rsum, start=True,
              stop=True, sem=cm.cT)
    p.wait("vector", cm.cT, vt)
    p.op("vector", "tensor_copy", cm.a_bc[:], cm.bank(2), sem=cm.cV)
    vv = p.op("vector", "tensor_copy", cm.b_bc[:], cm.bank(3), sem=cm.cV)
    p.wait("gpsimd", cm.cV, vv)
    last_add = {}
    for c in range(NCH):
        par = c % 2
        dst = cm.scr[:, par * TT:(par + 1) * TT]
        if c >= 2:
            p.wait("gpsimd", cm.cV, last_add[par])
        vp = p.op("gpsimd", "tensor_tensor", dst, srcfn(c), cm.a_bc[:], ALU.mult,
                  sem=cm.cP)
        p.wait("vector", cm.cP, vp)
        last_add[par] = p.op("vector", "tensor_tensor", cm.xln(c), dst, cm.b_bc[:],
                             ALU.add, sem=cm.cV)
    return cm.cV.n


def emit_xln_colsum(cm):
    p = cm.p
    p.wait("tensor", cm.cV, cm.cV.n)
    for k in range(NCH):
        vt = p.op("tensor", "matmul", cm.bank(7, part=1), cm.ones_cb[:], cm.xln(k),
                  start=(k == 0), stop=(k == NCH - 1), sem=cm.cT)
    p.wait("vector", cm.cT, vt)
    vv = p.op("vector", "tensor_copy", cm.mrow[0:1, 0:512], cm.bank(7, part=1),
              sem=cm.cV)
    p.wait("tensor", cm.cV, vv)
    return vv


def emit_attn_proj_fm(cm, wt_ap, slab, bias_cols, scale, out_cb, xln_ready,
                      pre, nout=NCH, banks=(0, 1)):
    p = cm.p
    p.wait("tensor", cm.cV, xln_ready)
    p.wait("gpsimd", pre[0], pre[1])
    for oc in range(nout):
        t = oc % 2
        sl = slab[:, t * NCH * 128:(t + 1) * NCH * 128]
        if oc >= 2:
            p.wait("gpsimd", cm.cT, cm.cT.n)
        vq = p.dma("gpsimd", sl,
                   wt_ap.rearrange("(k p) h -> p k h", p=128)[:, :, oc * 128:(oc + 1) * 128],
                   cm.qG)
        bnk = banks[oc % 2]
        p.wait("tensor", cm.qG, vq)
        for k in range(NCH):
            vt = p.op("tensor", "matmul", cm.bank(bnk), sl[:, k * 128:(k + 1) * 128],
                      cm.xln(k), start=(k == 0), stop=(k == NCH - 1), sem=cm.cT)
        p.wait("vector", cm.cT, vt)
        vs = p.op("vector", "tensor_scalar", out_cb(oc), cm.bank(bnk), scale,
                  bias_cols[:, oc:oc + 1], ALU.mult, ALU.add, sem=cm.cV)
        p.wait("tensor", cm.cV, vs)
    return cm.cV.n


def emit_attn_vproj_tm(cm, wt_ap, slab, brow, scale, vtmp, xln_ready, pre):
    p = cm.p
    p.wait("tensor", cm.cV, xln_ready)
    p.wait("gpsimd", pre[0], pre[1])
    for dw in range(4):
        t = dw % 2
        sl = slab[:, t * NCH * 512:(t + 1) * NCH * 512]
        if dw >= 2:
            p.wait("gpsimd", cm.cT, cm.cT.n)
        vq = p.dma("gpsimd", sl,
                   wt_ap.rearrange("(k p) h -> p k h", p=128)[:, :, dw * 512:(dw + 1) * 512],
                   cm.qG)
        p.wait("tensor", cm.qG, vq)
        for tc in range(4):
            bnk = 4 + tc % 2
            for k in range(NCH):
                p.op("tensor", "matmul", cm.bank(bnk),
                     cm.xln(k)[:, tc * 128:(tc + 1) * 128],
                     sl[:, k * 512:(k + 1) * 512],
                     start=(k == 0), stop=False, sem=cm.cT)
            vt = p.op("tensor", "matmul", cm.bank(bnk), cm.ones_rb[:],
                      brow[0:1, dw * 512:(dw + 1) * 512], start=False, stop=True,
                      sem=cm.cT)
            p.wait("scalar", cm.cT, vt)
            vs = p.op("scalar", "activation",
                      vtmp[:, tc * 2048 + dw * 512: tc * 2048 + (dw + 1) * 512],
                      cm.bank(bnk), AF.Copy, scale=scale, sem=cm.cS)
            p.wait("tensor", cm.cS, vs)
    return cm.cS.n


def emit_flash(cm, layer, mypos0, nheads, kTh_load, vext_load, qT_dram, ctxT_dram,
               flb):
    p = cm.p
    causal = (layer == 0)
    nm = cm.cols[:, 8:9]
    rl = cm.cols[:, 9:10]
    for h in range(nheads):
        par = h % 2
        p.wait("sync", cm.cT, cm.cT.n)
        kv_ready = []
        if kTh_load is not None:
            kv_ready.append((cm.qY, kTh_load(h, par)))
        if vext_load is not None:
            kv_ready.append((cm.qS, vext_load(h, par)))
        kT = flb.kT[par] if kTh_load is not None else flb.kT[0]
        vext = flb.vext[par] if vext_load is not None else flb.vext[0]
        vq = p.dma("sync", flb.qT[par][:], qT_dram[h * 128:(h + 1) * 128, :], cm.qY)
        kv_ready.append((cm.qY, vq))
        for s, v in kv_ready:
            p.wait("tensor", s, v)
        for hq in range(2):
            for j in range(4):
                J = 4 * hq + j
                nkb = J + 1 if causal else 8
                p.wait("tensor", cm.cV, cm.cV.n)
                for b in range(nkb):
                    vt = p.op("tensor", "matmul", cm.bank(b),
                              flb.qT[par][:, hq * 512 + j * 128: hq * 512 + (j + 1) * 128],
                              kT[:, b * 512:(b + 1) * 512], start=True, stop=True,
                              sem=cm.cT)
                if causal:
                    p.wait("vector", cm.cT, vt)
                    p.op("vector", "tensor_tensor", cm.bank(J), cm.bank(J),
                         flb.maskQ[:], ALU.add, sem=cm.cV)
                    p.seq("vector", cm.cV)
                else:
                    p.wait("vector", cm.cT, vt)
                p.op("vector", "tensor_reduce", nm, cm.ps[:, 0:nkb * 512], AX.X,
                     ALU.max, negate=True, sem=cm.cV)
                p.seq("vector", cm.cV)
                vv = cm.cV.n
                p.wait("tensor", cm.cV, vv)
                vt = p.op("tensor", "matmul", cm.bank(6, width=128, part=1), nm,
                          cm.ident[:], start=True, stop=True, sem=cm.cT)
                p.wait("scalar", cm.cT, vt)
                vs = p.op("scalar", "activation",
                          cm.mrow[0:1, hq * 512 + j * 128: hq * 512 + (j + 1) * 128],
                          cm.bank(6, width=128, part=1), AF.Copy, sem=cm.cS)
                p.wait("tensor", cm.cS, vs)
                p.wait("vector", cm.cS, vs)
            nkb128 = 16 * hq + 16 if causal else 32
            pt_used = {}
            first_kb = True
            for kb in range(nkb128):
                tpar = kb % 2
                bA = tpar
                pt = flb.PT[:, tpar * 512:(tpar + 1) * 512]
                p.wait("tensor", cm.cS, cm.cS.n)
                p.op("tensor", "matmul", cm.bank(bA),
                     kT[:, kb * 128:(kb + 1) * 128],
                     flb.qT[par][:, hq * 512:(hq + 1) * 512], start=True, stop=False,
                     sem=cm.cT)
                vt = p.op("tensor", "matmul", cm.bank(bA), cm.ones_rb[:],
                          cm.mrow[0:1, hq * 512:(hq + 1) * 512], start=False,
                          stop=True, sem=cm.cT)
                if causal and kb >= 16 * hq:
                    r = kb - 16 * hq
                    p.wait("vector", cm.cT, vt)
                    vv = p.op("vector", "tensor_tensor", cm.bank(bA), cm.bank(bA),
                              flb.maskT[:, r * 512:(r + 1) * 512], ALU.add, sem=cm.cV)
                    p.wait("scalar", cm.cV, vv)
                p.wait("scalar", cm.cT, vt)
                if tpar in pt_used:
                    p.wait("scalar", cm.cT, pt_used[tpar])
                vs = p.op("scalar", "activation", pt, cm.bank(bA), AF.Exp,
                          scale=cm.kc[:, 1:2], sem=cm.cS)
                p.wait("tensor", cm.cS, vs)
                for j in range(4):
                    vt = p.op("tensor", "matmul", cm.bank(2 + j, width=129),
                              pt[:, j * 128:(j + 1) * 128],
                              vext[:, kb * 132:kb * 132 + 129],
                              start=first_kb, stop=(kb == nkb128 - 1), sem=cm.cT)
                pt_used[tpar] = vt
                first_kb = False
            for j in range(4):
                pv = cm.bank(2 + j, width=129)
                p.wait("vector", cm.cT, cm.cT.n)
                p.op("vector", "reciprocal", rl, pv[:, 128:129], sem=cm.cV)
                p.seq("vector", cm.cV)
                vv = p.op("vector", "tensor_scalar_mul", flb.ytile[:], pv[:, 0:128],
                          rl[:, 0:1], sem=cm.cV)
                p.wait("tensor", cm.cV, vv)
                vt = p.op("tensor", "matmul", cm.bank(6, width=128), flb.ytile[:],
                          cm.ident[:], start=True, stop=True, sem=cm.cT)
                p.wait("scalar", cm.cT, vt)
                vs = p.op("scalar", "activation",
                          flb.ctmp[:, hq * 512 + j * 128: hq * 512 + (j + 1) * 128],
                          cm.bank(6, width=128), AF.Copy, sem=cm.cS)
                p.wait("tensor", cm.cS, vs)
                p.wait("vector", cm.cS, vs)
        p.wait("sync", cm.cS, cm.cS.n)
        vo = p.dma("sync", ctxT_dram[h * 128:(h + 1) * 128, :], flb.ctmp[:], cm.qY)
        p.wait("scalar", cm.qY, vo)
    return cm.qY.n


def emit_attn_oproj(cm, layer, wt_o_ap, oslab, ctx_tile, ctxT_dram, bo_cols,
                    pre, zerop):
    p = cm.p
    scale = 0.01 if layer == 1 else 1.0
    p.wait("gpsimd", pre[0], pre[1])
    for tt in range(NTT):
        if tt > 0:
            p.wait("gpsimd", cm.cT, cm.cT.n)
        vq = p.dma("gpsimd", ctx_tile[:].rearrange("p (c t) -> p c t", c=NCH),
                   ctxT_dram.rearrange("(c p) t -> p c t", p=128)[:, :, tt * TT:(tt + 1) * TT],
                   cm.qG)
        p.wait("tensor", cm.qG, vq)
        if zerop:
            for k in range(NCH):
                vt = p.op("tensor", "matmul", cm.bank(4, part=1), cm.ones_cb[:],
                          ctx_tile[:, k * TT:(k + 1) * TT], start=(k == 0),
                          stop=(k == NCH - 1), sem=cm.cT)
            p.wait("vector", cm.cT, vt)
            vv = p.op("vector", "tensor_copy", cm.mrow[0:1, 0:512],
                      cm.bank(4, part=1), sem=cm.cV)
            p.wait("tensor", cm.cV, vv)
        for oc in range(NCH):
            t = oc % 2
            sl = oslab[:, t * NCH * 128:(t + 1) * NCH * 128]
            if oc >= 2 or tt > 0:
                p.wait("gpsimd", cm.cT, cm.cT.n)
            vq = p.dma("gpsimd", sl,
                       wt_o_ap.rearrange("(k p) h -> p k h", p=128)[:, :, oc * 128:(oc + 1) * 128],
                       cm.qG)
            bnk = oc % 2
            p.wait("tensor", cm.qG, vq)
            for k in range(NCH):
                p.op("tensor", "matmul", cm.bank(bnk), sl[:, k * 128:(k + 1) * 128],
                     ctx_tile[:, k * TT:(k + 1) * TT], start=(k == 0),
                     stop=(not zerop and k == NCH - 1), sem=cm.cT)
            if zerop:
                vt = p.op("tensor", "matmul", cm.bank(bnk), cm.negz_rb[0:1, 0:128],
                          cm.mrow[0:1, 0:512], start=False, stop=True, sem=cm.cT)
            else:
                vt = cm.cT.n
            p.wait("vector", cm.cT, vt)
            par = oc % 2
            tmp = cm.sgsu[:, par * TT:(par + 1) * TT]
            vs = p.op("vector", "tensor_scalar", tmp, cm.bank(bnk), scale,
                      bo_cols[:, oc:oc + 1], ALU.mult, ALU.add, sem=cm.cV)
            p.seq("vector", cm.cV)
            vv = p.op("vector", "scalar_tensor_tensor", cm.xc(oc, tt), tmp,
                      cm.kc[:, 3:4], cm.xc(oc, tt), ALU.mult, ALU.add, sem=cm.cV)
            p.wait("tensor", cm.cV, vv)
    return cm.cV.n


class NS:
    pass


def make_attn_masks(qoff):
    i = np.arange(128)[:, None]
    kk = np.arange(512)[None, :]
    mQ = np.where(kk <= 4 * i + qoff, 0.0, NEG).astype(ml_dtypes.bfloat16)
    ki = np.arange(128)[:, None]
    qq = np.arange(512)[None, :]
    mT = np.stack([np.where(ki <= 4 * qq + qoff - 128 * rb, 0.0, NEG)
                   for rb in range(16)]).astype(ml_dtypes.bfloat16)
    return mQ, mT


def emit_sweep_load(cm, sx, xT_bat, s, tt, after_T):
    p = cm.p
    p.wait("gpsimd", cm.cT, after_T)
    col = s * 1024 + tt * 512
    return p.dma("gpsimd", sx[:].rearrange("p (c t) -> p c t", c=NCH),
                 xT_bat.rearrange("(c p) t -> p c t", p=128)[:, :, col:col + 512],
                 cm.qG)


def emit_attn_l1(cm, xT_bat, wts, bias_srcs, scr_names, pre, ph_unused):
    p, nc = cm.p, cm.nc
    kT_all, v_all, qT_dram, ctxT_dram = (scr_names["kT_all"], scr_names["v_all"],
                                         scr_names["qT"], scr_names["ctxT"])
    bq = nc.alloc_sbuf_tensor("l1a_bq", [128, NCH], F32)
    bk = nc.alloc_sbuf_tensor("l1a_bk", [128, NCH], F32)
    bo = nc.alloc_sbuf_tensor("l1a_bo", [128, NCH], F32)
    p.wait("gpsimd", pre[0], pre[1])
    p.wait("tensor", pre[0], pre[1])
    cm.dbg_bq = bq[:]
    emit_scale_cols(cm, bias_srcs["qb_q"], NCH, bq, fold=1e-4)
    emit_scale_cols(cm, bias_srcs["qb_k"], NCH, bk, fold=1e-4)
    emit_scale_cols(cm, bias_srcs["qb_o"], NCH, bo, fold=1e-4)
    ph_b = Phase(nc)
    brow_v = ph_b.sb("l1a_brv", [1, D], BF16)
    emit_brow(cm, bias_srcs["qb_v"], brow_v, 0.01)
    bias_ready = cm.cV.n
    sx = ph_b.sb("l1a_sx", [128, NCH * TT], F32)
    ktmp = ph_b.sb("l1a_kt", [128, 2 * TT], BF16)
    vtmp = ph_b.sb("l1a_vt", [128, 2 * D], BF16)
    slab = ph_b.sb("l1a_sl", [128, 2 * NCH * 128], BF16)
    vslab = ph_b.sb("l1a_vsl", [128, 2 * NCH * 256], BF16)
    st_done = {}

    def kq_out(dst_dram, col):
        def cb(oc):
            par = oc % 2
            tile = ktmp[:, par * TT:(par + 1) * TT]
            if ("k", par) in st_done:
                p.wait("scalar", cm.qY, st_done[("k", par)])
            return tile
        return cb

    def kq_store(dst_dram, col, oc):
        par = oc % 2
        tile = ktmp[:, par * TT:(par + 1) * TT]
        p.wait("sync", cm.cS, cm.cS.n)
        st_done[("k", par)] = p.dma(
            "sync", dst_dram.rearrange("(c p) t -> p c t", p=128)[:, oc:oc + 1, col:col + 512],
            tile, cm.qY)

    for s in range(4):
        for tt2 in range(2):
            col = s * 1024 + tt2 * 512
            vx = emit_sweep_load(cm, sx, xT_bat, s, tt2, cm.cT.n)
            xr = emit_ln_src_tt(cm, lambda c: sx[:, c * TT:(c + 1) * TT],
                                xwait=((cm.qG, vx), (cm.cV, bias_ready),
                                       pre))
            p.wait("tensor", cm.cV, xr)
            emit_xln_colsum(cm)
            p.wait("gpsimd", pre[0], pre[1])
            for oc in range(NCH):
                t = oc % 2
                sl = slab[:, t * NCH * 128:(t + 1) * NCH * 128]
                if oc >= 2:
                    p.wait("gpsimd", cm.cT, cm.cT.n)
                vq = p.dma("gpsimd", sl,
                           wts["wt_k"].rearrange("(k p) h -> p k h", p=128)[:, :, oc * 128:(oc + 1) * 128],
                           cm.qG)
                bnk = oc % 2
                p.wait("tensor", cm.qG, vq)
                for k in range(NCH):
                    vt = p.op("tensor", "matmul", cm.bank(bnk),
                              sl[:, k * 128:(k + 1) * 128], cm.xln(k),
                              start=(k == 0), stop=False, sem=cm.cT)
                vt = p.op("tensor", "matmul", cm.bank(bnk), cm.negz_rb[0:1, 0:128],
                          cm.mrow[0:1, 0:512], start=False, stop=True, sem=cm.cT)
                p.wait("vector", cm.cT, vt)
                par = oc % 2
                tile = ktmp[:, par * TT:(par + 1) * TT]
                if ("k", par) in st_done:
                    p.wait("vector", cm.qY, st_done[("k", par)])
                vs = p.op("vector", "tensor_scalar", tile, cm.bank(bnk), 0.01,
                          bk[:, oc:oc + 1], ALU.mult, ALU.add, sem=cm.cV)
                p.wait("tensor", cm.cV, vs)
                p.wait("sync", cm.cV, cm.cV.n)
                st_done[("k", par)] = p.dma(
                    "sync", kT_all.rearrange("(c p) t -> p c t", p=128)[:, oc:oc + 1, col:col + 512],
                    tile, cm.qY)
            p.wait("tensor", cm.cV, xr)
            for dw in range(8):
                t = dw % 2
                vsl = vslab[:, t * NCH * 256:(t + 1) * NCH * 256]
                if dw >= 2:
                    p.wait("gpsimd", cm.cT, cm.cT.n)
                vq = p.dma("gpsimd", vsl,
                           wts["wt_v"].rearrange("(k p) h -> p k h", p=128)[:, :, dw * 256:(dw + 1) * 256],
                           cm.qG)
                p.wait("tensor", cm.qG, vq)
                for tc in range(4):
                    bnk = 4 + tc % 2
                    for k in range(NCH):
                        p.op("tensor", "matmul", cm.bank(bnk, width=256),
                             cm.xln(k)[:, tc * 128:(tc + 1) * 128],
                             vsl[:, k * 256:(k + 1) * 256],
                             start=(k == 0), stop=False, sem=cm.cT)
                    p.op("tensor", "matmul", cm.bank(bnk, width=256),
                         cm.mrow[0:1, tc * 128:(tc + 1) * 128],
                         cm.negz_rb[0:1, 0:256], start=False, stop=False, sem=cm.cT)
                    vt = p.op("tensor", "matmul", cm.bank(bnk, width=256),
                              cm.ones_rb[:], brow_v[0:1, dw * 256:(dw + 1) * 256],
                              start=False, stop=True, sem=cm.cT)
                    p.wait("scalar", cm.cT, vt)
                    par = tc % 2
                    vtile = vtmp[:, par * D + dw * 256: par * D + (dw + 1) * 256]
                    if ("v", par, dw) in st_done:
                        p.wait("scalar", cm.qY, st_done[("v", par, dw)])
                    vs = p.op("scalar", "activation", vtile,
                              cm.bank(bnk, width=256), AF.Copy,
                              scale=cm.kc[:, 2:3], sem=cm.cS)
                    p.wait("tensor", cm.cS, vs)
                    p.wait("sync", cm.cS, cm.cS.n)
                    st_done[("v", par, dw)] = p.dma(
                        "sync", v_all[col + tc * 128:col + (tc + 1) * 128,
                                      dw * 256:(dw + 1) * 256], vtile, cm.qY)
    for tt2 in range(2):
        col = tt2 * 512
        xr = emit_ln_tt(cm, tt2)
        p.wait("tensor", cm.cV, xr)
        emit_xln_colsum(cm)
        for oc in range(NCH):
            t = oc % 2
            sl = slab[:, t * NCH * 128:(t + 1) * NCH * 128]
            p.wait("gpsimd", cm.cT, cm.cT.n)
            vq = p.dma("gpsimd", sl,
                       wts["wt_q"].rearrange("(k p) h -> p k h", p=128)[:, :, oc * 128:(oc + 1) * 128],
                       cm.qG)
            bnk = oc % 2
            p.wait("tensor", cm.qG, vq)
            for k in range(NCH):
                vt = p.op("tensor", "matmul", cm.bank(bnk),
                          sl[:, k * 128:(k + 1) * 128], cm.xln(k),
                          start=(k == 0), stop=False, sem=cm.cT)
            vt = p.op("tensor", "matmul", cm.bank(bnk), cm.negz_rb[0:1, 0:128],
                      cm.mrow[0:1, 0:512], start=False, stop=True, sem=cm.cT)
            p.wait("vector", cm.cT, vt)
            par = oc % 2
            tile = ktmp[:, par * TT:(par + 1) * TT]
            if ("k", par) in st_done:
                p.wait("vector", cm.qY, st_done[("k", par)])
            vs = p.op("vector", "tensor_scalar", tile, cm.bank(bnk), 0.01,
                      bq[:, oc:oc + 1], ALU.mult, ALU.add, sem=cm.cV)
            p.wait("tensor", cm.cV, vs)
            p.wait("sync", cm.cV, cm.cV.n)
            st_done[("k", par)] = p.dma(
                "sync", qT_dram.rearrange("(c p) t -> p c t", p=128)[:, oc:oc + 1, col:col + 512],
                tile, cm.qY)
    p.wait("sync", cm.qY, cm.qY.n)
    p.wait("gpsimd", cm.qY, cm.qY.n)
    p.wait("scalar", cm.qY, cm.qY.n)
    ph_b.close()
    ph_c = Phase(nc)
    flb = NS()
    flb.kT = [ph_c.sb("l1a_kTh0", [128, SEQ], BF16), ph_c.sb("l1a_kTh1", [128, SEQ], BF16)]
    flb.vext = [ph_c.sb("l1a_vx0", [128, 32 * 132], BF16),
                ph_c.sb("l1a_vx1", [128, 32 * 132], BF16)]
    flb.qT = [ph_c.sb("l1a_q0", [128, TOK], BF16), ph_c.sb("l1a_q1", [128, TOK], BF16)]
    flb.PT = ph_c.sb("l1a_pt", [128, 2 * 512], BF16)
    flb.ctmp = ph_c.sb("l1a_ctx", [128, TOK], BF16)
    flb.ytile = ph_c.sb("l1a_yt", [128, 128], F32)
    p.op("gpsimd", "memset", flb.vext[0][:], 1.0, sem=cm.cP)
    p.op("gpsimd", "memset", flb.vext[1][:], 1.0, sem=cm.cP)
    p.wait("sync", cm.cP, cm.cP.n)
    p.wait("scalar", cm.cP, cm.cP.n)
    kv_done = cm.qY.n

    def kth_load(h, par):
        p.wait("sync", cm.qY, kv_done)
        return p.dma("sync", flb.kT[par][:], kT_all[h * 128:(h + 1) * 128, :], cm.qY)

    def vext_load(h, par):
        p.wait("scalar", cm.qY, kv_done)
        p.wait("scalar", cm.cT, cm.cT.n)
        return p.dma("scalar",
                     flb.vext[par][:].rearrange("p (b s) -> p b s", b=32)[:, :, 0:128],
                     v_all.rearrange("(b p) d -> p b d", p=128)[:, :, h * 128:(h + 1) * 128],
                     cm.qS)

    emit_flash(cm, 1, 0, NH, kth_load, vext_load, qT_dram, ctxT_dram, flb)
    p.wait("gpsimd", cm.qY, cm.qY.n)
    p.wait("sync", cm.qS, cm.qS.n)
    ph_c.close()
    ph_d = Phase(nc)
    ctx_tile = ph_d.sb("l1a_cx", [128, NCH * TT], BF16)
    oslab = ph_d.sb("l1a_osl", [128, 2 * NCH * 128], BF16)
    emit_attn_oproj(cm, 1, wts["wt_o"], oslab, ctx_tile, ctxT_dram, bo,
                    pre, zerop=True)
    ph_d.close()


def emit_attn_l0(cm, xT_bat, wlat_bf, wd_bf, wo_bf, b_lat, b_d, b_pj, scr,
                 cast_ready, ph_unused):
    p, nc = cm.p, cm.nc
    qT_dram, ctxT_dram = scr["qT"], scr["ctxT"]
    blat = nc.alloc_sbuf_tensor("l0a_bl", [128, 1], F32)
    bd = nc.alloc_sbuf_tensor("l0a_bd", [128, NCH], F32)
    bpj = nc.alloc_sbuf_tensor("l0a_bp", [128, NCH], F32)
    emit_scale_cols(cm, b_lat, 1, blat)
    emit_scale_cols(cm, b_d, NCH, bd)
    emit_scale_cols(cm, b_pj, NCH, bpj)
    bias_ready = cm.cV.n
    ph_c = Phase(nc)
    flb = NS()
    flb.kT = [ph_c.sb("l0a_kT", [128, SEQ], BF16)]
    flb.vext = [ph_c.sb("l0a_vx", [128, 32 * 132], BF16)]
    flb.qT = [ph_c.sb("l0a_q0", [128, TOK], BF16), ph_c.sb("l0a_q1", [128, TOK], BF16)]
    flb.PT = ph_c.sb("l0a_pt", [128, 2 * 512], BF16)
    flb.ctmp = ph_c.sb("l0a_ctx", [128, TOK], BF16)
    flb.ytile = ph_c.sb("l0a_yt", [128, 128], F32)
    flb.maskQ = ph_c.sb("l0a_mq", [128, 512], BF16)
    flb.maskT = ph_c.sb("l0a_mt", [128, 16 * 512], BF16)
    identb = ph_c.sb("l0a_idb", [128, 128], BF16)
    kslab = ph_c.sb("l0a_ksl", [128, NCH * 128], BF16)
    vq = p.dma("gpsimd", flb.maskQ[:], cm.consts_aps["maskQc"], cm.qG)
    vq = p.dma("gpsimd", flb.maskT[:].rearrange("p (b s) -> p b s", b=16),
               cm.consts_aps["maskTc"].rearrange("b p s -> p b s"), cm.qG)
    vq = p.dma("gpsimd", identb[:], cm.consts_aps["identb"], cm.qG)
    p.wait("gpsimd", cm.qG, cast_ready)
    vq = p.dma("gpsimd", kslab[:], wlat_bf.rearrange("(k p) h -> p k h", p=128),
               cm.qG)
    p.wait("vector", cm.qG, vq)
    p.wait("tensor", cm.qG, vq)
    ph_b = Phase(nc)
    sx = ph_b.sb("l0a_sx", [128, NCH * TT], F32)
    qslab = ph_b.sb("l0a_qsl", [128, 2 * NCH * 128], BF16)
    qtmp = ph_b.sb("l0a_qt", [128, 2 * TT], BF16)
    st_done = {}
    for s in range(4):
        for tt2 in range(2):
            vx = emit_sweep_load(cm, sx, xT_bat, s, tt2, cm.cT.n)
            xr = emit_ln_src_tt(cm, lambda c: sx[:, c * TT:(c + 1) * TT],
                                xwait=((cm.qG, vx), (cm.cV, bias_ready),
                                       (cm.qG, cast_ready)))
            p.wait("tensor", cm.cV, xr)
            col = s * 1024 + tt2 * 512
            for k in range(NCH):
                vt = p.op("tensor", "matmul", cm.bank(4),
                          kslab[:, k * 128:(k + 1) * 128], cm.xln(k),
                          start=(k == 0), stop=(k == NCH - 1), sem=cm.cT)
            p.wait("vector", cm.cT, vt)
            vs = p.op("vector", "tensor_scalar_add", flb.kT[0][:, col:col + 512],
                      cm.bank(4), blat[:, 0:1], sem=cm.cV)
            p.wait("tensor", cm.cV, vs)
    for tt2 in range(2):
        col = tt2 * 512
        xr = emit_ln_tt(cm, tt2)
        p.wait("tensor", cm.cV, xr)
        p.wait("gpsimd", cm.qG, cast_ready)
        for oc in range(NCH):
            t = oc % 2
            sl = qslab[:, t * NCH * 128:(t + 1) * NCH * 128]
            p.wait("gpsimd", cm.cT, cm.cT.n)
            vq = p.dma("gpsimd", sl,
                       wd_bf.rearrange("(k p) h -> p k h", p=128)[:, :, oc * 128:(oc + 1) * 128],
                       cm.qG)
            bnk = oc % 2
            p.wait("tensor", cm.qG, vq)
            for k in range(NCH):
                vt = p.op("tensor", "matmul", cm.bank(bnk),
                          sl[:, k * 128:(k + 1) * 128], cm.xln(k),
                          start=(k == 0), stop=(k == NCH - 1), sem=cm.cT)
            p.wait("vector", cm.cT, vt)
            par = oc % 2
            tile = qtmp[:, par * TT:(par + 1) * TT]
            if ("q", par) in st_done:
                p.wait("vector", cm.qY, st_done[("q", par)])
            vs = p.op("vector", "tensor_scalar_add", tile, cm.bank(bnk),
                      bd[:, oc:oc + 1], sem=cm.cV)
            p.wait("tensor", cm.cV, vs)
            p.wait("sync", cm.cV, cm.cV.n)
            st_done[("q", par)] = p.dma(
                "sync", qT_dram.rearrange("(c p) t -> p c t", p=128)[:, oc:oc + 1, col:col + 512],
                tile, cm.qY)
    p.wait("sync", cm.qY, cm.qY.n)
    p.wait("gpsimd", cm.qY, cm.qY.n)
    p.wait("scalar", cm.qY, cm.qY.n)
    ph_b.close()
    p.op("gpsimd", "memset", flb.vext[0][:], 1.0, sem=cm.cP)
    p.wait("tensor", cm.cP, cm.cP.n)
    p.wait("tensor", cm.cS, cm.cS.n)
    for kb in range(32):
        vt = p.op("tensor", "matmul", cm.bank(6, width=128),
                  flb.kT[0][:, kb * 128:(kb + 1) * 128], identb[:],
                  start=True, stop=True, sem=cm.cT)
        p.wait("scalar", cm.cT, vt)
        vs = p.op("scalar", "activation", flb.vext[0][:, kb * 132:kb * 132 + 128],
                  cm.bank(6, width=128), AF.Copy, sem=cm.cS)
        p.wait("tensor", cm.cS, vs)
    emit_flash(cm, 0, 0, NH, None, None, qT_dram, ctxT_dram, flb)
    p.wait("gpsimd", cm.qY, cm.qY.n)
    ph_c.close()
    ph_d = Phase(nc)
    ctx_tile = ph_d.sb("l0a_cx", [128, NCH * TT], BF16)
    oslab = ph_d.sb("l0a_osl", [128, 2 * NCH * 128], BF16)
    emit_attn_oproj(cm, 0, wo_bf, oslab, ctx_tile, ctxT_dram, bpj,
                    (cm.qG, cast_ready), zerop=False)
    ph_d.close()


def build_l0attn_launch():
    nc = bass.Bass("TRN2", target_bir_lowering=False)
    xT = nc.dram_tensor("xT", [D, TOK], F32, kind="ExternalInput").ap()
    xT_bat = nc.dram_tensor("xT_bat", [D, SEQ], F32, kind="ExternalInput").ap()
    wlat = nc.dram_tensor("L0_attn_W_lat", [D, DL], F32, kind="ExternalInput").ap()
    blat = nc.dram_tensor("L0_attn_b_lat", [DL], F32, kind="ExternalInput").ap()
    wd = nc.dram_tensor("L0_attn_W_d", [D, D], F32, kind="ExternalInput").ap()
    bd = nc.dram_tensor("L0_attn_b_d", [D], F32, kind="ExternalInput").ap()
    wpj = nc.dram_tensor("L0_attn_W_proj", [D, D], F32, kind="ExternalInput").ap()
    bpj = nc.dram_tensor("L0_attn_b_proj", [D], F32, kind="ExternalInput").ap()
    consts = decl_consts(nc, attn=True, masks=True)
    out = nc.dram_tensor("xT_out", [D, TOK], F32, kind="ExternalOutput").ap()
    wlat_bf = nc.dram_tensor("wlat_bf", [D, DL], BF16).ap()
    wd_bf = nc.dram_tensor("wd_bf", [D, D], BF16).ap()
    wo_bf = nc.dram_tensor("wo_bf", [D, D], BF16).ap()
    qT = nc.dram_tensor("qT_scr", [D, TOK], BF16).ap()
    ctxT = nc.dram_tensor("ctxT_scr", [D, TOK], BF16).ap()

    p = P(nc)
    cm = Common(p, nc, consts)
    cm.wait_init()
    vx = p.dma("sync", cm.X[:].rearrange("p (c t) -> p c t", c=NCH),
               xT.rearrange("(c p) t -> p c t", p=128), cm.qY)
    cm.xload = vx
    p.dma("gpsimd", wlat_bf, wlat, cm.qG)
    p.dma("gpsimd", wd_bf, wd, cm.qG)
    cast_ready = p.dma("gpsimd", wo_bf, wpj, cm.qG)
    ph = Phase(nc)
    p.wait("tensor", cm.qY, vx)
    p.wait("scalar", cm.qY, vx)
    p.wait("gpsimd", cm.qY, vx)
    emit_attn_l0(cm, xT_bat, wlat_bf, wd_bf, wo_bf, blat, bd, bpj,
                 {"qT": qT, "ctxT": ctxT}, cast_ready, ph)
    ph.close()
    sync_output_store(cm, out)
    p.finalize()
    return nc


def build_l1attn_launch():
    nc = bass.Bass("TRN2", target_bir_lowering=False)
    xT = nc.dram_tensor("xT", [D, TOK], F32, kind="ExternalInput").ap()
    xT_bat = nc.dram_tensor("xT_bat", [D, SEQ], F32, kind="ExternalInput").ap()
    qwq = nc.dram_tensor("L1_attn_qw_q", [D, D], I32, kind="ExternalInput").ap()
    qwk = nc.dram_tensor("L1_attn_qw_k", [D, D], I32, kind="ExternalInput").ap()
    qwv = nc.dram_tensor("L1_attn_qw_v", [D, D], I32, kind="ExternalInput").ap()
    qwo = nc.dram_tensor("L1_attn_qw_o", [D, D], I32, kind="ExternalInput").ap()
    qbq = nc.dram_tensor("L1_attn_qb_q", [D], I32, kind="ExternalInput").ap()
    qbk = nc.dram_tensor("L1_attn_qb_k", [D], I32, kind="ExternalInput").ap()
    qbv = nc.dram_tensor("L1_attn_qb_v", [D], I32, kind="ExternalInput").ap()
    qbo = nc.dram_tensor("L1_attn_qb_o", [D], I32, kind="ExternalInput").ap()
    consts = decl_consts(nc, attn=True, masks=False)
    out = nc.dram_tensor("xT_out", [D, TOK], F32, kind="ExternalOutput").ap()
    scs = {nm: nc.dram_tensor(f"sc_{nm}", [D, D], BF16).ap() for nm in "qkvo"}
    wts = {f"wt_{nm}": nc.dram_tensor(f"wt_{nm}", [D, D], BF16).ap() for nm in "qkvo"}
    kT_all = nc.dram_tensor("kT_all", [D, SEQ], BF16).ap()
    v_all = nc.dram_tensor("v_all", [SEQ, D], BF16).ap()
    qT = nc.dram_tensor("qT_scr", [D, TOK], BF16).ap()
    ctxT = nc.dram_tensor("ctxT_scr", [D, TOK], BF16).ap()

    p = P(nc)
    cm = Common(p, nc, consts)
    cm.wait_init()
    vx = p.dma("sync", cm.X[:].rearrange("p (c t) -> p c t", c=NCH),
               xT.rearrange("(c p) t -> p c t", p=128), cm.qY)
    cvs = emit_cast_weights(cm, [qwq, qwk, qwv, qwo],
                            [scs["q"], scs["k"], scs["v"], scs["o"]])
    wt_ready = emit_transpose_weights(
        cm, [(scs[nm], wts[f"wt_{nm}"], cv) for nm, cv in zip("qkvo", cvs)],
        tallR=D)
    ph = Phase(nc)
    p.wait("tensor", cm.qY, vx)
    p.wait("scalar", cm.qY, vx)
    p.wait("gpsimd", cm.qY, vx)
    emit_attn_l1(cm, xT_bat, wts,
                 {"qb_q": qbq, "qb_k": qbk, "qb_v": qbv, "qb_o": qbo},
                 {"kT_all": kT_all, "v_all": v_all, "qT": qT, "ctxT": ctxT},
                 (cm.qS, wt_ready), ph)
    ph.close()
    sync_output_store(cm, out)
    p.finalize()
    return nc


_NC_CACHE = {}


def _get_nc(name):
    if name not in _NC_CACHE:
        _NC_CACHE[name] = {
            "l0ffn": build_l0ffn_launch,
            "l1ffn": build_l1ffn_launch,
            "l0attn": build_l0attn_launch,
            "l1attn": build_l1attn_launch,
        }[name]()
    return _NC_CACHE[name]


def _shard(xT_full, c):
    b, q = c // 4, c % 4
    return np.ascontiguousarray(xT_full[:, b * SEQ + q:(b + 1) * SEQ:4])


def _gather(slices):
    full = np.empty((D, 2 * SEQ), np.float32)
    for c in range(8):
        b, q = c // 4, c % 4
        full[:, b * SEQ + q:(b + 1) * SEQ:4] = slices[c]
    return full


def kernel(**inputs):
    consts = make_consts()
    masks = [make_attn_masks(c % 4) for c in range(8)]
    x = np.asarray(inputs["L0_attn_x"], dtype=np.float32)
    xT = np.ascontiguousarray(x.reshape(2 * SEQ, D).T)

    def run(name, extra_fn):
        nc = _get_nc(name)
        in_maps = []
        for c in range(8):
            m = {"xT": _shard(xT, c), "c_ident": consts["c_ident"]}
            m.update(extra_fn(c))
            in_maps.append(m)
        res = run_bass_kernel_spmd(nc, in_maps, core_ids=list(range(8)))
        return _gather([r["xT_out"] for r in res.results])

    l0ffn_in = lambda c: {"L0_ffn_w_fc": inputs["L0_ffn_w_fc"],
                          "L0_ffn_w_proj": inputs["L0_ffn_w_proj"]}
    l1ffn_in = lambda c: {k: inputs[k] for k in (
        "L1_ffn_gate_w", "L1_ffn_gate_scale", "L1_ffn_up_w", "L1_ffn_up_scale",
        "L1_ffn_down_w", "L1_ffn_down_scale")}

    def l0attn_in(c):
        b = c // 4
        m = {k: inputs[k] for k in ("L0_attn_W_lat", "L0_attn_b_lat",
                                    "L0_attn_W_d", "L0_attn_b_d",
                                    "L0_attn_W_proj", "L0_attn_b_proj")}
        m["xT_bat"] = np.ascontiguousarray(xT[:, b * SEQ:(b + 1) * SEQ])
        m["c_identb"] = consts["c_identb"]
        m["c_maskQc"], m["c_maskTc"] = masks[c]
        return m

    def l1attn_in(c):
        b = c // 4
        m = {k: inputs[k] for k in ("L1_attn_qw_q", "L1_attn_qw_k", "L1_attn_qw_v",
                                    "L1_attn_qw_o", "L1_attn_qb_q", "L1_attn_qb_k",
                                    "L1_attn_qb_v", "L1_attn_qb_o")}
        m["xT_bat"] = np.ascontiguousarray(xT[:, b * SEQ:(b + 1) * SEQ])
        m["c_identb"] = consts["c_identb"]
        return m

    xT = run("l0ffn", l0ffn_in)
    xT = run("l0attn", l0attn_in)
    xT = run("l0ffn", l0ffn_in)
    xT = run("l1ffn", l1ffn_in)
    xT = run("l1attn", l1attn_in)
    xT = run("l1ffn", l1ffn_in)
    return np.ascontiguousarray(xT.T).reshape(2, SEQ, D)


# version 04
# speedup vs baseline: 1.3532x; 1.3532x over previous
import numpy as np
import ml_dtypes
import concourse.bass as bass
import concourse.mybir as mybir
from concourse.bass_utils import run_bass_kernel_spmd

F32 = mybir.dt.float32
BF16 = mybir.dt.bfloat16
I32 = mybir.dt.int32
AF = mybir.ActivationFunctionType
ALU = mybir.AluOpType
AX = mybir.AxisListType

NCORE = 8
D = 2048
NCH = D // 128
TOK = 1024
TT = 512
NTT = TOK // TT
H0 = 5120
NH0 = H0 // 128
I1 = 8192
NI1 = I1 // 128
NH = 16
DL = 128
SEQ = 4096
NEG = -1.0e30

ENGS = ("sync", "scalar", "tensor", "vector", "gpsimd")


class SemT:
    __slots__ = ("h", "n", "name")

    def __init__(self, h, name):
        self.h = h
        self.n = 0
        self.name = name


SERIAL = False


class P:
    def __init__(self, nc):
        self.nc = nc
        self.ops = {e: [] for e in ENGS}
        self.all_sems = []
        self.dsem = {}

    def sem(self, name):
        s = SemT(self.nc.alloc_semaphore(name), name)
        self.all_sems.append(s)
        return s

    def op(self, eng, meth, *args, sem=None, inc=1, **kw):
        import traceback
        _st = traceback.extract_stack()[-3]
        if SERIAL:
            for s in self.all_sems:
                if s.n > 0:
                    self.wait(eng, s, s.n)
            if sem is None:
                if eng not in self.dsem:
                    self.dsem[eng] = self.sem(f"dser_{eng}")
                sem = self.dsem[eng]
                inc = 16 if meth == "dma_start" else 1
        def fn(e, meth=meth, args=args, kw=kw, sem=sem, inc=inc, _st=_st):
            try:
                ins = getattr(e, meth)(*args, **kw)
            except Exception as ex:
                raise RuntimeError(f"op {meth} from {_st.filename}:{_st.lineno}: {ex}") from ex
            if sem is not None:
                ins.then_inc(sem.h, inc)
        self.ops[eng].append(fn)
        if sem is not None:
            sem.n += inc
            return sem.n
        return None

    def dma(self, eng, out, in_, sem, **kw):
        return self.op(eng, "dma_start", sem=sem, inc=16, out=out, in_=in_, **kw)

    def wait(self, eng, sem, val=None):
        if val is None:
            val = sem.n
        if val <= 0:
            return
        assert val <= sem.n, f"wait {sem.name}>={val} but builds to {sem.n}"
        def fn(e, sem=sem, v=val):
            e.wait_ge(sem.h, v)
        self.ops[eng].append(fn)

    def seq(self, eng, sem):
        self.wait(eng, sem, sem.n)

    def finalize(self):
        with self.nc.Block() as block:
            for eng in ENGS:
                deco = getattr(block, eng)
                ops = self.ops[eng]
                def body(e, ops=ops):
                    for f in ops:
                        f(e)
                deco(body)


class Phase:
    def __init__(self, nc):
        self.nc = nc
        self.guards = []

    def sb(self, name, shape, dtype):
        g = self.nc.sbuf_tensor(name, shape, dtype, side="right")
        h = g.__enter__()
        self.guards.append(g)
        return h

    def close(self):
        for g in reversed(self.guards):
            g.__exit__(None, None, None)
        self.guards = []


def make_consts():
    ident = np.eye(128, dtype=np.float32)
    return {"c_ident": ident, "c_identb": ident.astype(ml_dtypes.bfloat16)}


def decl_consts(nc, attn=False, masks=False):
    d = {
        "ident": nc.dram_tensor("c_ident", [128, 128], F32, kind="ExternalInput").ap(),
    }
    if attn:
        d["identb"] = nc.dram_tensor("c_identb", [128, 128], BF16, kind="ExternalInput").ap()
    if masks:
        d["maskQc"] = nc.dram_tensor("c_maskQc", [128, 512], BF16, kind="ExternalInput").ap()
        d["maskTc"] = nc.dram_tensor("c_maskTc", [16, 128, 512], BF16, kind="ExternalInput").ap()
    return d


class Common:
    def __init__(self, p, nc, consts_aps):
        self.p = p
        self.nc = nc
        self.X = nc.alloc_sbuf_tensor("X", [128, NCH * TOK], F32)
        self.XLN = nc.alloc_sbuf_tensor("XLN", [128, NCH * TT], BF16)
        self.a_bc = nc.alloc_sbuf_tensor("a_bc", [128, TT], F32)
        self.b_bc = nc.alloc_sbuf_tensor("b_bc", [128, TT], F32)
        self.row = nc.alloc_sbuf_tensor("row", [1, 2048], F32)
        self.mrow = nc.alloc_sbuf_tensor("mrow", [1, TOK], BF16)
        self.stat = nc.alloc_sbuf_tensor("stat", [1, 32], F32)
        self.cols = nc.alloc_sbuf_tensor("cols", [128, 16], F32)
        self.scr = nc.alloc_sbuf_tensor("scr", [128, 2 * TT], F32)
        self.sgsu = nc.alloc_sbuf_tensor("sgsu", [128, 2 * TT], BF16)
        self.srow = nc.alloc_sbuf_tensor("srow", [1, 512], F32)
        self.ones_c = nc.alloc_sbuf_tensor("ones_c", [128, 1], F32)
        self.ones_r = nc.alloc_sbuf_tensor("ones_r", [1, 128], F32)
        self.negz_rb = nc.alloc_sbuf_tensor("negz_rb", [1, 512], BF16)
        self.ones_rb = nc.alloc_sbuf_tensor("ones_rb", [1, 128], BF16)
        self.ones_cb = nc.alloc_sbuf_tensor("ones_cb", [128, 1], BF16)
        self.ident = nc.alloc_sbuf_tensor("ident", [128, 128], F32)
        self.kc = nc.alloc_sbuf_tensor("kc", [128, 4], F32)
        self.ps = nc.alloc_psum_tensor("ps", [128, 8 * 512], F32)
        self.qG = p.sem("qG")
        self.qS = p.sem("qS")
        self.qY = p.sem("qY")
        self.cT = p.sem("cT")
        self.cV = p.sem("cV")
        self.cS = p.sem("cS")
        self.cP = p.sem("cP")
        self.cc = p.sem("cc")
        self.consts_aps = consts_aps
        p.dma("gpsimd", self.ident[:], consts_aps["ident"], self.qG)
        p.op("gpsimd", "memset", self.ones_c[:], 1.0, sem=self.cP)
        p.op("gpsimd", "memset", self.ones_r[:], 1.0, sem=self.cP)
        p.op("gpsimd", "memset", self.ones_cb[:], 1.0, sem=self.cP)
        p.op("gpsimd", "memset", self.negz_rb[:], -128.0, sem=self.cP)
        p.op("gpsimd", "memset", self.kc[:, 0:1], 1e-5, sem=self.cP)
        p.op("gpsimd", "memset", self.kc[:, 1:2], 0.08838834764831845, sem=self.cP)
        p.op("gpsimd", "memset", self.kc[:, 2:3], 0.01, sem=self.cP)
        p.op("gpsimd", "memset", self.kc[:, 3:4], 1.0, sem=self.cP)
        p.op("gpsimd", "memset", self.ones_rb[:], 1.0, sem=self.cP)
        self.initq, self.initp = self.qG.n, self.cP.n

    def wait_init(self):
        for eng in ("tensor", "scalar", "vector", "sync"):
            self.p.wait(eng, self.cP, self.initp)
            self.p.wait(eng, self.qG, self.initq)

    def bank(self, b, width=512, part=128):
        return self.ps[0:part, b * 512: b * 512 + width]

    def xc(self, c, tt=None):
        if tt is None:
            return self.X[:, c * TOK:(c + 1) * TOK]
        return self.X[:, c * TOK + tt * TT: c * TOK + (tt + 1) * TT]

    def xln(self, c):
        return self.XLN[:, c * TT:(c + 1) * TT]


def emit_ln_tt(cm, tt, xwait=()):
    p = cm.p
    for eng in ("tensor", "scalar", "gpsimd"):
        for s, v in xwait:
            p.wait(eng, s, v)
    p.wait("scalar", cm.cV, cm.cV.n)
    p.wait("scalar", cm.cP, cm.cP.n)
    rsum = cm.row[0:1, 0:512]
    rstd = cm.row[0:1, 512:1024]
    nmr = cm.row[0:1, 1024:1536]
    rsq = cm.row[0:1, 1536:2048]
    for c in range(NCH):
        p.op("tensor", "matmul", cm.bank(0, part=1), cm.ones_c[:], cm.xc(c, tt),
             start=(c == 0), stop=(c == NCH - 1), sem=cm.cT)
    sq_used = {}
    sq_v = []
    for c in range(NCH + 1):
        if c < NCH:
            par = c % 2
            sq = cm.scr[:, par * TT:(par + 1) * TT]
            if c >= 2:
                p.wait("scalar", cm.cT, sq_used[par])
            sq_v.append(p.op("scalar", "activation", sq, cm.xc(c, tt), AF.Square,
                             sem=cm.cS))
        else:
            sq_v.append(p.op("scalar", "activation", cm.stat[0:1, 30:31],
                             cm.stat[0:1, 29:30], AF.Square, sem=cm.cS))
        if c >= 1:
            cc = c - 1
            parc = cc % 2
            sqc = cm.scr[:, parc * TT:(parc + 1) * TT]
            p.wait("tensor", cm.cS, sq_v[c])
            sq_used[parc] = p.op("tensor", "matmul", cm.bank(1, part=1),
                                 cm.ones_c[:], sqc, start=(cc == 0),
                                 stop=(cc == NCH - 1), sem=cm.cT)
    p.wait("vector", cm.cT, cm.cT.n)
    p.op("vector", "tensor_scalar_mul", rsum, cm.bank(0, part=1), 1.0 / D, sem=cm.cV)
    p.op("vector", "tensor_scalar_mul", rsq, cm.bank(1, part=1), 1.0 / D, sem=cm.cV)
    p.seq("vector", cm.cV)
    p.op("vector", "tensor_mul", nmr, rsum, rsum, sem=cm.cV)
    p.seq("vector", cm.cV)
    vv = p.op("vector", "tensor_sub", rstd, rsq, nmr, sem=cm.cV)
    p.wait("scalar", cm.cV, vv)
    vs = p.op("scalar", "activation", rsq, rstd, AF.Sqrt, bias=cm.kc[0:1, 0:1],
              sem=cm.cS)
    p.wait("vector", cm.cS, vs)
    p.op("vector", "reciprocal", rstd, rsq, sem=cm.cV)
    p.seq("vector", cm.cV)
    p.op("vector", "tensor_scalar_mul", nmr, rsum, -1.0, sem=cm.cV)
    p.seq("vector", cm.cV)
    vv = p.op("vector", "tensor_mul", rsum, nmr, rstd, sem=cm.cV)
    p.wait("tensor", cm.cV, vv)
    p.op("tensor", "matmul", cm.bank(2), cm.ones_r[:], rstd, start=True, stop=True,
         sem=cm.cT)
    vt = p.op("tensor", "matmul", cm.bank(3), cm.ones_r[:], rsum, start=True,
              stop=True, sem=cm.cT)
    p.wait("vector", cm.cT, vt)
    p.op("vector", "tensor_copy", cm.a_bc[:], cm.bank(2), sem=cm.cV)
    vv = p.op("vector", "tensor_copy", cm.b_bc[:], cm.bank(3), sem=cm.cV)
    p.wait("gpsimd", cm.cV, vv)
    last_add = {}
    for c in range(NCH):
        par = c % 2
        dst = cm.scr[:, par * TT:(par + 1) * TT]
        if c >= 2:
            p.wait("gpsimd", cm.cV, last_add[par])
        vp = p.op("gpsimd", "tensor_tensor", dst, cm.xc(c, tt), cm.a_bc[:], ALU.mult,
                  sem=cm.cP)
        p.wait("vector", cm.cP, vp)
        last_add[par] = p.op("vector", "tensor_tensor", cm.xln(c), dst, cm.b_bc[:],
                             ALU.add, sem=cm.cV)
    return cm.cV.n


def emit_tern_prep(cm, w_fc, w_proj, sg_fc, sg_pj, wt_fc, wt_pj):
    p, nc = cm.p, cm.nc
    ph = Phase(nc)
    wbuf = ph.sb("tern_wbuf", [128, 2 * 2048], F32)
    sa = ph.sb("tern_sa", [128, 2048], BF16)
    sb = ph.sb("tern_sb", [128, 2048], BF16)
    s2 = ph.sb("tern_s2", [128, 2 * 2048], BF16)
    part = ph.sb("tern_part", [128, 128], F32)
    specs = [("fc", w_fc, 5120, 2048, sg_fc, wt_fc, 0),
             ("pj", w_proj, 2048, 5120, sg_pj, wt_pj, 1)]
    pcol = 0
    pbase = {}
    red_done = {}
    for (nm, w, R, C, sg, wt, widx) in specs:
        pbase[nm] = pcol
        for r in range(R // 128):
            for j in range((C + 2047) // 2048):
                cw = min(2048, C - j * 2048)
                b = pcol % 2
                buf = wbuf[:, b * 2048: b * 2048 + cw]
                if ("prev", b) in red_done:
                    p.wait("gpsimd", cm.cV, red_done[("prev", b)])
                v = p.dma("gpsimd", buf, w[r * 128:(r + 1) * 128, j * 2048:j * 2048 + cw],
                          cm.qG)
                p.wait("vector", cm.qG, v)
                red_done[("prev", b)] = p.op(
                    "vector", "tensor_reduce", part[:, pcol:pcol + 1], buf, AX.X,
                    ALU.add, apply_absolute_value=True, sem=cm.cV)
                pcol += 1
        p.seq("vector", cm.cV)
        p.op("vector", "tensor_reduce", part[:, 120 + widx:121 + widx],
             part[:, pbase[nm]:pcol], AX.X, ALU.add, sem=cm.cV)
    p.wait("tensor", cm.cV, cm.cV.n)
    p.op("tensor", "matmul", cm.bank(0, width=1, part=1), part[:, 120:121],
         cm.ones_c[:], start=True, stop=True, sem=cm.cT)
    vt = p.op("tensor", "matmul", cm.bank(1, width=1, part=1), part[:, 121:122],
              cm.ones_c[:], start=True, stop=True, sem=cm.cT)
    p.wait("vector", cm.cT, vt)
    st = cm.stat
    p.op("vector", "tensor_copy", st[0:1, 0:1], cm.bank(0, width=1, part=1), sem=cm.cV)
    p.op("vector", "tensor_copy", st[0:1, 1:2], cm.bank(1, width=1, part=1), sem=cm.cV)
    p.seq("vector", cm.cV)
    p.op("vector", "tensor_scalar", st[0:1, 2:3], st[0:1, 0:1], 1.0 / (5120 * 2048),
         1e-5, ALU.mult, ALU.max, sem=cm.cV)
    p.op("vector", "tensor_scalar", st[0:1, 3:4], st[0:1, 1:2], 1.0 / (5120 * 2048),
         1e-5, ALU.mult, ALU.max, sem=cm.cV)
    p.seq("vector", cm.cV)
    p.op("vector", "tensor_scalar_mul", st[0:1, 4:5], st[0:1, 2:3], 0.7, sem=cm.cV)
    p.op("vector", "tensor_scalar_mul", st[0:1, 5:6], st[0:1, 2:3], -0.7, sem=cm.cV)
    p.op("vector", "tensor_scalar_mul", st[0:1, 6:7], st[0:1, 3:4], 0.7, sem=cm.cV)
    p.op("vector", "tensor_scalar_mul", st[0:1, 7:8], st[0:1, 3:4], -0.7, sem=cm.cV)
    p.op("vector", "tensor_mul", st[0:1, 8:9], st[0:1, 2:3], st[0:1, 2:3], sem=cm.cV)
    p.seq("vector", cm.cV)
    p.op("vector", "tensor_mul", st[0:1, 9:10], st[0:1, 8:9], st[0:1, 3:4], sem=cm.cV)
    p.seq("vector", cm.cV)
    vv = p.op("vector", "tensor_scalar_mul", st[0:1, 10:11], st[0:1, 9:10], 0.0625,
              sem=cm.cV)
    p.wait("tensor", cm.cV, vv)
    for (si, di) in [(4, 0), (5, 1), (6, 2), (7, 3), (10, 4)]:
        p.op("tensor", "matmul", cm.bank(2, width=1), cm.ones_r[:], st[0:1, si:si + 1],
             start=True, stop=True, sem=cm.cT)
        p.wait("vector", cm.cT, cm.cT.n)
        vv = p.op("vector", "tensor_copy", cm.cols[:, di:di + 1], cm.bank(2, width=1),
                  sem=cm.cV)
        p.wait("tensor", cm.cV, vv)
    stats_done = cm.cV.n
    p.wait("scalar", cm.cV, stats_done)
    p.wait("gpsimd", cm.cV, stats_done)
    store_done = {}
    s2_done = {}
    ci = 0
    for (nm, w, R, C, sg, wt, widx) in specs:
        thr = cm.cols[:, 2 * widx:2 * widx + 1]
        nthr = cm.cols[:, 2 * widx + 1:2 * widx + 2]
        for r in range(R // 128):
            for j in range((C + 2047) // 2048):
                cw = min(2048, C - j * 2048)
                t = ci % 2
                ci += 1
                buf = wbuf[:, t * 2048: t * 2048 + cw]
                s2b = s2[:, t * 2048: t * 2048 + cw]
                if ("sb", t) in s2_done:
                    p.wait("gpsimd", cm.cS, s2_done[("sb", t)])
                v = p.dma("gpsimd", buf, w[r * 128:(r + 1) * 128, j * 2048:j * 2048 + cw],
                          cm.qG)
                p.wait("scalar", cm.qG, v)
                if "va" in s2_done:
                    p.wait("scalar", cm.cV, s2_done["va"])
                p.op("scalar", "activation", sa[:, 0:cw], buf, AF.Sign, bias=nthr,
                     sem=cm.cS)
                vs = p.op("scalar", "activation", sb[:, 0:cw], buf, AF.Sign, bias=thr,
                          sem=cm.cS)
                s2_done[("sb", t)] = vs
                p.wait("vector", cm.cS, vs)
                if ("st", t) in store_done:
                    p.wait("vector", cm.qS, store_done[("st", t)])
                va = p.op("vector", "tensor_tensor", s2b, sa[:, 0:cw], sb[:, 0:cw],
                          ALU.add, sem=cm.cV)
                s2_done["va"] = va
                p.wait("scalar", cm.cV, va)
                store_done[("st", t)] = p.dma(
                    "scalar", sg[r * 128:(r + 1) * 128, j * 2048:j * 2048 + cw], s2b,
                    cm.qS)
    ph.close()
    cm.dbg_pass2_done = cm.qS.n
    p.wait("sync", cm.qS, cm.qS.n)
    ph2 = Phase(nc)
    tbuf = ph2.sb("tern_tbuf", [128, 2 * 5120], BF16)
    tdone = {}
    for (nm, w, R, C, sg, wt, widx) in specs:
        for j in range(C // 128):
            t = j % 2
            tb = tbuf[:, t * 5120: t * 5120 + R]
            if ("tp", t) in tdone:
                p.wait("sync", cm.qS, tdone[("tp", t)])
            v = p.dma("sync", tb, sg[0:R, j * 128:(j + 1) * 128], cm.qY, transpose=True)
            p.wait("sync", cm.qY, v)
            tdone[("tp", t)] = p.dma("sync", wt[j * 128:(j + 1) * 128, 0:R], tb,
                                     cm.qS)
    ph2.close()
    p.wait("gpsimd", cm.qS, cm.qS.n)
    return cm.qS.n


def emit_cast_weights(cm, raws, scratches):
    p = cm.p
    return [p.dma("gpsimd", sc, raw, cm.qG) for raw, sc in zip(raws, scratches)]


def emit_transpose_weights(cm, items, tallR):
    p, nc = cm.p, cm.nc
    ph = Phase(nc)
    cm.ntw = getattr(cm, "ntw", 0) + 1
    tbuf = ph.sb(f"w_tbuf{cm.ntw}", [128, tallR], BF16)
    for (_, _, cv) in items:
        p.wait("sync", cm.qG, cv)
    for (sc, wt, cv) in items:
        R, C = sc.shape
        for j in range(C // 128):
            tb = tbuf[:, 0:R]
            v = p.dma("sync", tb, sc[0:R, j * 128:(j + 1) * 128], cm.qY,
                      transpose=True)
            p.wait("sync", cm.qY, v)
            vs = p.dma("sync", wt[j * 128:(j + 1) * 128, 0:R], tb, cm.qS)
            p.wait("sync", cm.qS, vs)
    ready = cm.qS.n
    ph.close()
    return ready


def emit_scale_cols(cm, scale_ap_1d, n, dst, fold=None):
    p = cm.p
    row2 = scale_ap_1d.rearrange("(a b) -> a b", a=1)
    for seg in range((n * 128 + 511) // 512):
        lo = seg * 512
        hi = min(n * 128, lo + 512)
        p.wait("gpsimd", cm.cT, cm.cT.n)
        vq = p.dma("gpsimd", cm.srow[0:1, 0:hi - lo], row2[0:1, lo:hi], cm.qG)
        p.wait("tensor", cm.qG, vq)
        for c in range(lo // 128, hi // 128):
            p.op("tensor", "matmul", cm.bank(7, width=1),
                 cm.srow[0:1, c * 128 - lo:(c + 1) * 128 - lo], cm.ones_c[0:1, 0:1],
                 start=True, stop=True, sem=cm.cT)
            p.wait("vector", cm.cT, cm.cT.n)
            if fold is None:
                vv = p.op("vector", "tensor_copy", dst[:, c:c + 1],
                          cm.bank(7, width=1), sem=cm.cV)
            else:
                vv = p.op("vector", "tensor_scalar_mul", dst[:, c:c + 1],
                          cm.bank(7, width=1), fold, sem=cm.cV)
            p.wait("tensor", cm.cV, vv)
    return cm.cV.n


def emit_ffn_l0_tt(cm, tt, wt_fc, wt_pj, wt_ready, xln_ready, hbuf, fslab, pslab):
    p = cm.p
    p.wait("gpsimd", cm.qS, wt_ready)
    p.wait("tensor", cm.cV, xln_ready)
    p.wait("scalar", cm.cV, xln_ready)
    guard = {}
    for hc in range(NH0):
        t = hc % 2
        sl = fslab[:, t * NCH * 128:(t + 1) * NCH * 128]
        if hc >= 2:
            p.wait("gpsimd", cm.cT, cm.cT.n)
        vq = p.dma("gpsimd", sl,
                   wt_fc.rearrange("(k p) h -> p k h", p=128)[:, :, hc * 128:(hc + 1) * 128],
                   cm.qG)
        bnk = hc % 2
        p.wait("tensor", cm.qG, vq)
        for k in range(NCH):
            vt = p.op("tensor", "matmul", cm.bank(bnk), sl[:, k * 128:(k + 1) * 128],
                      cm.xln(k), start=(k == 0), stop=(k == NCH - 1), sem=cm.cT)
        p.wait("scalar", cm.cT, vt)
        par = hc % 2
        rb = cm.sgsu[:, par * TT:(par + 1) * TT]
        if ("h", par) in guard:
            p.wait("scalar", cm.cV, guard[("h", par)])
        vs = p.op("scalar", "activation", rb, cm.bank(bnk), AF.Relu, sem=cm.cS)
        p.wait("vector", cm.cS, vs)
        guard[("h", par)] = p.op("vector", "tensor_mul", hbuf[:, hc * TT:(hc + 1) * TT],
                                 rb, rb, sem=cm.cV)
        p.wait("tensor", cm.cS, vs)
    p.wait("vector", cm.cV, cm.cV.n)
    for dc in range(NCH):
        t = dc % 2
        sl = pslab[:, t * NH0 * 128:(t + 1) * NH0 * 128]
        if dc >= 2:
            p.wait("gpsimd", cm.cT, cm.cT.n)
        vq = p.dma("gpsimd", sl,
                   wt_pj.rearrange("(k p) h -> p k h", p=128)[:, :, dc * 128:(dc + 1) * 128],
                   cm.qG)
        bnk = 2 + dc % 2
        p.wait("tensor", cm.qG, vq)
        p.wait("tensor", cm.cV, cm.cV.n)
        for k in range(NH0):
            vt = p.op("tensor", "matmul", cm.bank(bnk), sl[:, k * 128:(k + 1) * 128],
                      hbuf[:, k * TT:(k + 1) * TT], start=(k == 0), stop=(k == NH0 - 1),
                      sem=cm.cT)
        p.wait("vector", cm.cT, vt)
        vv = p.op("vector", "scalar_tensor_tensor", cm.xc(dc, tt), cm.bank(bnk),
                  cm.cols[:, 4:5], cm.xc(dc, tt), ALU.mult, ALU.add, sem=cm.cV)
        p.wait("tensor", cm.cV, vv)


def emit_ffn_l1_tt(cm, tt, wt_g, wt_u, wt_d, gsc, usc, dsc, pre_ready, xln_ready,
                   hbuf, gslab, uslab, dslab):
    p = cm.p
    p.wait("gpsimd", cm.qS, pre_ready)
    p.wait("tensor", cm.cV, xln_ready)
    p.wait("scalar", cm.cV, xln_ready)
    guard = {}
    for ic in range(NI1):
        t = ic % 2
        gsl = gslab[:, t * NCH * 128:(t + 1) * NCH * 128]
        usl = uslab[:, t * NCH * 128:(t + 1) * NCH * 128]
        if ic >= 2:
            p.wait("gpsimd", cm.cT, cm.cT.n)
        p.dma("gpsimd", gsl,
              wt_g.rearrange("(k p) h -> p k h", p=128)[:, :, ic * 128:(ic + 1) * 128],
              cm.qG)
        vq = p.dma("gpsimd", usl,
                   wt_u.rearrange("(k p) h -> p k h", p=128)[:, :, ic * 128:(ic + 1) * 128],
                   cm.qG)
        p.wait("tensor", cm.qG, vq)
        bg = ic % 2
        bu = 2 + ic % 2
        for k in range(NCH):
            vtg = p.op("tensor", "matmul", cm.bank(bg), gsl[:, k * 128:(k + 1) * 128],
                       cm.xln(k), start=(k == 0), stop=(k == NCH - 1), sem=cm.cT)
        for k in range(NCH):
            vtu = p.op("tensor", "matmul", cm.bank(bu), usl[:, k * 128:(k + 1) * 128],
                       cm.xln(k), start=(k == 0), stop=(k == NCH - 1), sem=cm.cT)
        par = ic % 2
        sg = cm.sgsu[:, par * TT:(par + 1) * TT]
        su = cm.scr[:, par * TT:(par + 1) * TT]
        if ("h", par) in guard:
            p.wait("scalar", cm.cV, guard[("h", par)])
        p.wait("scalar", cm.cT, vtg)
        p.op("scalar", "activation", sg, cm.bank(bg), AF.Silu,
             scale=gsc[:, ic:ic + 1], sem=cm.cS)
        p.wait("scalar", cm.cT, vtu)
        vs = p.op("scalar", "activation", su, cm.bank(bu), AF.Copy,
                  scale=usc[:, ic:ic + 1], sem=cm.cS)
        p.wait("vector", cm.cS, vs)
        guard[("h", par)] = p.op("vector", "tensor_mul", hbuf[:, ic * TT:(ic + 1) * TT],
                                 sg, su, sem=cm.cV)
        p.wait("tensor", cm.cS, vs)
    p.wait("tensor", cm.cV, cm.cV.n)
    for ic in range(NI1):
        vt = p.op("tensor", "matmul", cm.bank(4, part=1), cm.ones_cb[:],
                  hbuf[:, ic * TT:(ic + 1) * TT], start=(ic == 0),
                  stop=(ic == NI1 - 1), sem=cm.cT)
    p.wait("vector", cm.cT, vt)
    vv = p.op("vector", "tensor_copy", cm.mrow[0:1, 0:512], cm.bank(4, part=1),
              sem=cm.cV)
    p.wait("tensor", cm.cV, vv)
    NIQ = 16
    half_used = {}
    for dc in range(NCH):
        bnk = 5 + dc % 2
        for qt in range(NI1 // NIQ):
            hf = qt % 2
            dsl = dslab[:, hf * NIQ * 128:(hf + 1) * NIQ * 128]
            if hf in half_used:
                p.wait("gpsimd", cm.cT, half_used[hf])
            vq = p.dma("gpsimd", dsl,
                       wt_d.rearrange("(k p) h -> p k h", p=128)[:, qt * NIQ:(qt + 1) * NIQ, dc * 128:(dc + 1) * 128],
                       cm.qG)
            p.wait("tensor", cm.qG, vq)
            for k in range(NIQ):
                gk = qt * NIQ + k
                vt = p.op("tensor", "matmul", cm.bank(bnk),
                          dsl[:, k * 128:(k + 1) * 128],
                          hbuf[:, gk * TT:(gk + 1) * TT],
                          start=(gk == 0), stop=False, sem=cm.cT)
            half_used[hf] = vt
        vt = p.op("tensor", "matmul", cm.bank(bnk), cm.negz_rb[0:1, 0:128], cm.mrow[0:1, 0:512],
                  start=False, stop=True, sem=cm.cT)
        p.wait("vector", cm.cT, vt)
        vv = p.op("vector", "scalar_tensor_tensor", cm.xc(dc, tt), cm.bank(bnk),
                  dsc[:, dc:dc + 1], cm.xc(dc, tt), ALU.mult, ALU.add, sem=cm.cV)
        p.wait("tensor", cm.cV, vv)


def sync_output_store(cm, out):
    p = cm.p
    p.wait("sync", cm.cV, cm.cV.n)
    p.wait("sync", cm.cP, cm.cP.n)
    vo = p.dma("sync", out.rearrange("(c p) t -> p c t", p=128),
               cm.X[:].rearrange("p (c t) -> p c t", c=NCH), cm.qY)
    p.wait("sync", cm.qY, vo)


def build_l0ffn_launch(debug=False):
    nc = bass.Bass("TRN2", target_bir_lowering=False,
                   dynamic_dma_scratch_size=16384)
    xT = nc.dram_tensor("xT", [D, TOK], F32, kind="ExternalInput").ap()
    w_fc = nc.dram_tensor("L0_ffn_w_fc", [H0, D], F32, kind="ExternalInput").ap()
    w_pj = nc.dram_tensor("L0_ffn_w_proj", [D, H0], F32, kind="ExternalInput").ap()
    consts = decl_consts(nc)
    out = nc.dram_tensor("xT_out", [D, TOK], F32, kind="ExternalOutput").ap()
    sg_fc = nc.dram_tensor("sg_fc", [H0, D], BF16).ap()
    sg_pj = nc.dram_tensor("sg_pj", [D, H0], BF16).ap()
    wt_fc = nc.dram_tensor("wt_fc", [D, H0], BF16).ap()
    wt_pj = nc.dram_tensor("wt_pj", [H0, D], BF16).ap()

    p = P(nc)
    cm = Common(p, nc, consts)
    cm.wait_init()
    vx = p.dma("sync", cm.X[:].rearrange("p (c t) -> p c t", c=NCH),
               xT.rearrange("(c p) t -> p c t", p=128), cm.qY)
    wt_ready = emit_tern_prep(cm, w_fc, w_pj, sg_fc, sg_pj, wt_fc, wt_pj)
    ph = Phase(nc)
    hbuf = ph.sb("l0_h", [128, NH0 * TT], BF16)
    fslab = ph.sb("l0_fs", [128, 2 * NCH * 128], BF16)
    pslab = ph.sb("l0_ps", [128, 2 * NH0 * 128], BF16)
    for tt in range(NTT):
        xln_ready = emit_ln_tt(cm, tt, xwait=((cm.qY, vx),))
        emit_ffn_l0_tt(cm, tt, wt_fc, wt_pj, wt_ready, xln_ready, hbuf, fslab, pslab)
    ph.close()
    sync_output_store(cm, out)
    p.finalize()
    return nc


def build_l1ffn_launch(debug=False):
    nc = bass.Bass("TRN2", target_bir_lowering=False,
                   dynamic_dma_scratch_size=16384)
    xT = nc.dram_tensor("xT", [D, TOK], F32, kind="ExternalInput").ap()
    gw = nc.dram_tensor("L1_ffn_gate_w", [I1, D], I32, kind="ExternalInput").ap()
    gs = nc.dram_tensor("L1_ffn_gate_scale", [I1], F32, kind="ExternalInput").ap()
    uw = nc.dram_tensor("L1_ffn_up_w", [I1, D], I32, kind="ExternalInput").ap()
    us = nc.dram_tensor("L1_ffn_up_scale", [I1], F32, kind="ExternalInput").ap()
    dw = nc.dram_tensor("L1_ffn_down_w", [D, I1], I32, kind="ExternalInput").ap()
    ds = nc.dram_tensor("L1_ffn_down_scale", [D], F32, kind="ExternalInput").ap()
    consts = decl_consts(nc)
    out = nc.dram_tensor("xT_out", [D, TOK], F32, kind="ExternalOutput").ap()
    sc_g = nc.dram_tensor("sc_g", [I1, D], BF16).ap()
    sc_u = nc.dram_tensor("sc_u", [I1, D], BF16).ap()
    sc_d = nc.dram_tensor("sc_d", [D, I1], BF16).ap()
    wt_g = nc.dram_tensor("wt_g", [D, I1], BF16).ap()
    wt_u = nc.dram_tensor("wt_u", [D, I1], BF16).ap()
    wt_d = nc.dram_tensor("wt_d", [I1, D], BF16).ap()

    p = P(nc)
    cm = Common(p, nc, consts)
    cm.wait_init()
    vx = p.dma("sync", cm.X[:].rearrange("p (c t) -> p c t", c=NCH),
               xT.rearrange("(c p) t -> p c t", p=128), cm.qY)
    cvs = emit_cast_weights(cm, [gw, uw, dw], [sc_g, sc_u, sc_d])
    wt_ready = emit_transpose_weights(
        cm, list(zip([sc_g, sc_u, sc_d], [wt_g, wt_u, wt_d], cvs)), tallR=I1)
    gsc = nc.alloc_sbuf_tensor("gsc", [128, NI1], F32)
    usc = nc.alloc_sbuf_tensor("usc", [128, NI1], F32)
    dsc = nc.alloc_sbuf_tensor("dsc", [128, NCH], F32)
    emit_scale_cols(cm, gs, NI1, gsc)
    emit_scale_cols(cm, us, NI1, usc)
    sc_ready = emit_scale_cols(cm, ds, NCH, dsc, fold=0.5)
    ph = Phase(nc)
    hbuf = ph.sb("l1_h", [128, NI1 * TT], BF16)
    for tt in range(NTT):
        xln_ready = emit_ln_tt(cm, tt, xwait=((cm.qY, vx), (cm.cV, sc_ready)))
        ph2 = Phase(nc)
        gslab = ph2.sb(f"l1_gs{tt}", [128, 2 * NCH * 128], BF16)
        uslab = ph2.sb(f"l1_us{tt}", [128, 2 * NCH * 128], BF16)
        dslab = ph2.sb(f"l1_ds{tt}", [128, 2 * 16 * 128], BF16)
        emit_ffn_l1_tt(cm, tt, wt_g, wt_u, wt_d, gsc, usc, dsc, wt_ready, xln_ready,
                       hbuf, gslab, uslab, dslab)
        ph2.close()
    ph.close()
    sync_output_store(cm, out)
    p.finalize()
    return nc


def kernel(**inputs):
    raise NotImplementedError("under construction")


ISQ = 0.08838834764831845


def emit_brow(cm, qb_1d, dst_brow, scale):
    p = cm.p
    row2 = qb_1d.rearrange("(a b) -> a b", a=1)
    for seg in range(4):
        lo, hi = seg * 512, (seg + 1) * 512
        p.wait("gpsimd", cm.cV, cm.cV.n)
        vq = p.dma("gpsimd", cm.srow[0:1, :], row2[0:1, lo:hi], cm.qG)
        p.wait("vector", cm.qG, vq)
        p.op("vector", "tensor_scalar_mul", dst_brow[0:1, lo:hi], cm.srow[0:1, :],
             scale, sem=cm.cV)
    return cm.cV.n


def emit_ln_src_tt(cm, srcfn, xwait=(), src_bf16=False):
    p = cm.p
    ones = cm.ones_cb if src_bf16 else cm.ones_c
    for eng in ("tensor", "scalar", "gpsimd"):
        for s, v in xwait:
            p.wait(eng, s, v)
    p.wait("scalar", cm.cV, cm.cV.n)
    p.wait("scalar", cm.cP, cm.cP.n)
    rsum = cm.row[0:1, 0:512]
    rstd = cm.row[0:1, 512:1024]
    nmr = cm.row[0:1, 1024:1536]
    rsq = cm.row[0:1, 1536:2048]
    for c in range(NCH):
        p.op("tensor", "matmul", cm.bank(0, part=1), ones[:], srcfn(c),
             start=(c == 0), stop=(c == NCH - 1), sem=cm.cT)
    sq_used = {}
    sq_v = []
    for c in range(NCH + 1):
        if c < NCH:
            par = c % 2
            sq = cm.scr[:, par * TT:(par + 1) * TT]
            if c >= 2:
                p.wait("scalar", cm.cT, sq_used[par])
            sq_v.append(p.op("scalar", "activation", sq, srcfn(c), AF.Square,
                             sem=cm.cS))
        else:
            sq_v.append(p.op("scalar", "activation", cm.stat[0:1, 30:31],
                             cm.stat[0:1, 29:30], AF.Square, sem=cm.cS))
        if c >= 1:
            cc = c - 1
            parc = cc % 2
            sqc = cm.scr[:, parc * TT:(parc + 1) * TT]
            p.wait("tensor", cm.cS, sq_v[c])
            sq_used[parc] = p.op("tensor", "matmul", cm.bank(1, part=1),
                                 cm.ones_c[:], sqc, start=(cc == 0),
                                 stop=(cc == NCH - 1), sem=cm.cT)
    p.wait("vector", cm.cT, cm.cT.n)
    p.op("vector", "tensor_scalar_mul", rsum, cm.bank(0, part=1), 1.0 / D, sem=cm.cV)
    p.op("vector", "tensor_scalar_mul", rsq, cm.bank(1, part=1), 1.0 / D, sem=cm.cV)
    p.seq("vector", cm.cV)
    p.op("vector", "tensor_mul", nmr, rsum, rsum, sem=cm.cV)
    p.seq("vector", cm.cV)
    vv = p.op("vector", "tensor_sub", rstd, rsq, nmr, sem=cm.cV)
    p.wait("scalar", cm.cV, vv)
    vs = p.op("scalar", "activation", rsq, rstd, AF.Sqrt, bias=cm.kc[0:1, 0:1],
              sem=cm.cS)
    p.wait("vector", cm.cS, vs)
    p.op("vector", "reciprocal", rstd, rsq, sem=cm.cV)
    p.seq("vector", cm.cV)
    p.op("vector", "tensor_scalar_mul", nmr, rsum, -1.0, sem=cm.cV)
    p.seq("vector", cm.cV)
    vv = p.op("vector", "tensor_mul", rsum, nmr, rstd, sem=cm.cV)
    p.wait("tensor", cm.cV, vv)
    p.op("tensor", "matmul", cm.bank(2), cm.ones_r[:], rstd, start=True, stop=True,
         sem=cm.cT)
    vt = p.op("tensor", "matmul", cm.bank(3), cm.ones_r[:], rsum, start=True,
              stop=True, sem=cm.cT)
    p.wait("vector", cm.cT, vt)
    p.op("vector", "tensor_copy", cm.a_bc[:], cm.bank(2), sem=cm.cV)
    vv = p.op("vector", "tensor_copy", cm.b_bc[:], cm.bank(3), sem=cm.cV)
    p.wait("gpsimd", cm.cV, vv)
    last_add = {}
    for c in range(NCH):
        par = c % 2
        dst = cm.scr[:, par * TT:(par + 1) * TT]
        if c >= 2:
            p.wait("gpsimd", cm.cV, last_add[par])
        vp = p.op("gpsimd", "tensor_tensor", dst, srcfn(c), cm.a_bc[:], ALU.mult,
                  sem=cm.cP)
        p.wait("vector", cm.cP, vp)
        last_add[par] = p.op("vector", "tensor_tensor", cm.xln(c), dst, cm.b_bc[:],
                             ALU.add, sem=cm.cV)
    return cm.cV.n


def emit_xln_colsum(cm):
    p = cm.p
    p.wait("tensor", cm.cV, cm.cV.n)
    for k in range(NCH):
        vt = p.op("tensor", "matmul", cm.bank(7, part=1), cm.ones_cb[:], cm.xln(k),
                  start=(k == 0), stop=(k == NCH - 1), sem=cm.cT)
    p.wait("vector", cm.cT, vt)
    vv = p.op("vector", "tensor_copy", cm.mrow[0:1, 0:512], cm.bank(7, part=1),
              sem=cm.cV)
    p.wait("tensor", cm.cV, vv)
    return vv


def emit_attn_proj_fm(cm, wt_ap, slab, bias_cols, scale, out_cb, xln_ready,
                      pre, nout=NCH, banks=(0, 1)):
    p = cm.p
    p.wait("tensor", cm.cV, xln_ready)
    p.wait("gpsimd", pre[0], pre[1])
    for oc in range(nout):
        t = oc % 2
        sl = slab[:, t * NCH * 128:(t + 1) * NCH * 128]
        if oc >= 2:
            p.wait("gpsimd", cm.cT, cm.cT.n)
        vq = p.dma("gpsimd", sl,
                   wt_ap.rearrange("(k p) h -> p k h", p=128)[:, :, oc * 128:(oc + 1) * 128],
                   cm.qG)
        bnk = banks[oc % 2]
        p.wait("tensor", cm.qG, vq)
        for k in range(NCH):
            vt = p.op("tensor", "matmul", cm.bank(bnk), sl[:, k * 128:(k + 1) * 128],
                      cm.xln(k), start=(k == 0), stop=(k == NCH - 1), sem=cm.cT)
        p.wait("vector", cm.cT, vt)
        vs = p.op("vector", "tensor_scalar", out_cb(oc), cm.bank(bnk), scale,
                  bias_cols[:, oc:oc + 1], ALU.mult, ALU.add, sem=cm.cV)
        p.wait("tensor", cm.cV, vs)
    return cm.cV.n


def emit_attn_vproj_tm(cm, wt_ap, slab, brow, scale, vtmp, xln_ready, pre):
    p = cm.p
    p.wait("tensor", cm.cV, xln_ready)
    p.wait("gpsimd", pre[0], pre[1])
    for dw in range(4):
        t = dw % 2
        sl = slab[:, t * NCH * 512:(t + 1) * NCH * 512]
        if dw >= 2:
            p.wait("gpsimd", cm.cT, cm.cT.n)
        vq = p.dma("gpsimd", sl,
                   wt_ap.rearrange("(k p) h -> p k h", p=128)[:, :, dw * 512:(dw + 1) * 512],
                   cm.qG)
        p.wait("tensor", cm.qG, vq)
        for tc in range(4):
            bnk = 4 + tc % 2
            for k in range(NCH):
                p.op("tensor", "matmul", cm.bank(bnk),
                     cm.xln(k)[:, tc * 128:(tc + 1) * 128],
                     sl[:, k * 512:(k + 1) * 512],
                     start=(k == 0), stop=False, sem=cm.cT)
            vt = p.op("tensor", "matmul", cm.bank(bnk), cm.ones_rb[:],
                      brow[0:1, dw * 512:(dw + 1) * 512], start=False, stop=True,
                      sem=cm.cT)
            p.wait("scalar", cm.cT, vt)
            vs = p.op("scalar", "activation",
                      vtmp[:, tc * 2048 + dw * 512: tc * 2048 + (dw + 1) * 512],
                      cm.bank(bnk), AF.Copy, scale=scale, sem=cm.cS)
            p.wait("tensor", cm.cS, vs)
    return cm.cS.n


def emit_flash(cm, layer, mypos0, nheads, kTh_load, vext_load, qT_dram, ctxT_dram,
               flb):
    p = cm.p
    causal = (layer == 0)
    nm = cm.cols[:, 8:9]
    rl = cm.cols[:, 9:10]
    for h in range(nheads):
        par = h % 2
        p.wait("sync", cm.cT, cm.cT.n)
        kv_ready = []
        if kTh_load is not None:
            kv_ready.append((cm.qY, kTh_load(h, par)))
        if vext_load is not None:
            kv_ready.append((cm.qS, vext_load(h, par)))
        kT = flb.kT[par] if kTh_load is not None else flb.kT[0]
        vext = flb.vext[par] if vext_load is not None else flb.vext[0]
        vq = p.dma("sync", flb.qT[par][:], qT_dram[h * 128:(h + 1) * 128, :], cm.qY)
        kv_ready.append((cm.qY, vq))
        for s, v in kv_ready:
            p.wait("tensor", s, v)
        for hq in range(2):
            for j in range(4):
                J = 4 * hq + j
                nkb = J + 1 if causal else 8
                p.wait("tensor", cm.cV, cm.cV.n)
                for b in range(nkb):
                    vt = p.op("tensor", "matmul", cm.bank(b),
                              flb.qT[par][:, hq * 512 + j * 128: hq * 512 + (j + 1) * 128],
                              kT[:, b * 512:(b + 1) * 512], start=True, stop=True,
                              sem=cm.cT)
                if causal:
                    p.wait("vector", cm.cT, vt)
                    p.op("vector", "tensor_tensor", cm.bank(J), cm.bank(J),
                         flb.maskQ[:], ALU.add, sem=cm.cV)
                    p.seq("vector", cm.cV)
                else:
                    p.wait("vector", cm.cT, vt)
                p.op("vector", "tensor_reduce", nm, cm.ps[:, 0:nkb * 512], AX.X,
                     ALU.max, negate=True, sem=cm.cV)
                p.seq("vector", cm.cV)
                vv = cm.cV.n
                p.wait("tensor", cm.cV, vv)
                vt = p.op("tensor", "matmul", cm.bank(6, width=128, part=1), nm,
                          cm.ident[:], start=True, stop=True, sem=cm.cT)
                p.wait("scalar", cm.cT, vt)
                vs = p.op("scalar", "activation",
                          cm.mrow[0:1, hq * 512 + j * 128: hq * 512 + (j + 1) * 128],
                          cm.bank(6, width=128, part=1), AF.Copy, sem=cm.cS)
                p.wait("tensor", cm.cS, vs)
                p.wait("vector", cm.cS, vs)
            nkb128 = 16 * hq + 16 if causal else 32
            pt_used = {}
            first_kb = True
            for kb in range(nkb128):
                tpar = kb % 2
                bA = tpar
                pt = flb.PT[:, tpar * 512:(tpar + 1) * 512]
                p.wait("tensor", cm.cS, cm.cS.n)
                p.op("tensor", "matmul", cm.bank(bA),
                     kT[:, kb * 128:(kb + 1) * 128],
                     flb.qT[par][:, hq * 512:(hq + 1) * 512], start=True, stop=False,
                     sem=cm.cT)
                vt = p.op("tensor", "matmul", cm.bank(bA), cm.ones_rb[:],
                          cm.mrow[0:1, hq * 512:(hq + 1) * 512], start=False,
                          stop=True, sem=cm.cT)
                if causal and kb >= 16 * hq:
                    r = kb - 16 * hq
                    p.wait("vector", cm.cT, vt)
                    vv = p.op("vector", "tensor_tensor", cm.bank(bA), cm.bank(bA),
                              flb.maskT[:, r * 512:(r + 1) * 512], ALU.add, sem=cm.cV)
                    p.wait("scalar", cm.cV, vv)
                p.wait("scalar", cm.cT, vt)
                if tpar in pt_used:
                    p.wait("scalar", cm.cT, pt_used[tpar])
                vs = p.op("scalar", "activation", pt, cm.bank(bA), AF.Exp,
                          scale=cm.kc[:, 1:2], sem=cm.cS)
                p.wait("tensor", cm.cS, vs)
                for j in range(4):
                    vt = p.op("tensor", "matmul", cm.bank(2 + j, width=129),
                              pt[:, j * 128:(j + 1) * 128],
                              vext[:, kb * 132:kb * 132 + 129],
                              start=first_kb, stop=(kb == nkb128 - 1), sem=cm.cT)
                pt_used[tpar] = vt
                first_kb = False
            for j in range(4):
                pv = cm.bank(2 + j, width=129)
                p.wait("vector", cm.cT, cm.cT.n)
                p.op("vector", "reciprocal", rl, pv[:, 128:129], sem=cm.cV)
                p.seq("vector", cm.cV)
                vv = p.op("vector", "tensor_scalar_mul", flb.ytile[:], pv[:, 0:128],
                          rl[:, 0:1], sem=cm.cV)
                p.wait("tensor", cm.cV, vv)
                vt = p.op("tensor", "matmul", cm.bank(6, width=128), flb.ytile[:],
                          cm.ident[:], start=True, stop=True, sem=cm.cT)
                p.wait("scalar", cm.cT, vt)
                vs = p.op("scalar", "activation",
                          flb.ctmp[:, hq * 512 + j * 128: hq * 512 + (j + 1) * 128],
                          cm.bank(6, width=128), AF.Copy, sem=cm.cS)
                p.wait("tensor", cm.cS, vs)
                p.wait("vector", cm.cS, vs)
        p.wait("sync", cm.cS, cm.cS.n)
        vo = p.dma("sync", ctxT_dram[h * 128:(h + 1) * 128, :], flb.ctmp[:], cm.qY)
        p.wait("scalar", cm.qY, vo)
    return cm.qY.n


def emit_attn_oproj(cm, layer, wt_o_ap, oslab, ctx_tile, ctxT_dram, bo_cols,
                    pre, zerop):
    p = cm.p
    scale = 0.01 if layer == 1 else 1.0
    p.wait("gpsimd", pre[0], pre[1])
    for tt in range(NTT):
        if tt > 0:
            p.wait("gpsimd", cm.cT, cm.cT.n)
        vq = p.dma("gpsimd", ctx_tile[:].rearrange("p (c t) -> p c t", c=NCH),
                   ctxT_dram.rearrange("(c p) t -> p c t", p=128)[:, :, tt * TT:(tt + 1) * TT],
                   cm.qG)
        p.wait("tensor", cm.qG, vq)
        if zerop:
            for k in range(NCH):
                vt = p.op("tensor", "matmul", cm.bank(4, part=1), cm.ones_cb[:],
                          ctx_tile[:, k * TT:(k + 1) * TT], start=(k == 0),
                          stop=(k == NCH - 1), sem=cm.cT)
            p.wait("vector", cm.cT, vt)
            vv = p.op("vector", "tensor_copy", cm.mrow[0:1, 0:512],
                      cm.bank(4, part=1), sem=cm.cV)
            p.wait("tensor", cm.cV, vv)
        for oc in range(NCH):
            t = oc % 2
            sl = oslab[:, t * NCH * 128:(t + 1) * NCH * 128]
            if oc >= 2 or tt > 0:
                p.wait("gpsimd", cm.cT, cm.cT.n)
            vq = p.dma("gpsimd", sl,
                       wt_o_ap.rearrange("(k p) h -> p k h", p=128)[:, :, oc * 128:(oc + 1) * 128],
                       cm.qG)
            bnk = oc % 2
            p.wait("tensor", cm.qG, vq)
            for k in range(NCH):
                p.op("tensor", "matmul", cm.bank(bnk), sl[:, k * 128:(k + 1) * 128],
                     ctx_tile[:, k * TT:(k + 1) * TT], start=(k == 0),
                     stop=(not zerop and k == NCH - 1), sem=cm.cT)
            if zerop:
                vt = p.op("tensor", "matmul", cm.bank(bnk), cm.negz_rb[0:1, 0:128],
                          cm.mrow[0:1, 0:512], start=False, stop=True, sem=cm.cT)
            else:
                vt = cm.cT.n
            p.wait("vector", cm.cT, vt)
            par = oc % 2
            tmp = cm.sgsu[:, par * TT:(par + 1) * TT]
            vs = p.op("vector", "tensor_scalar", tmp, cm.bank(bnk), scale,
                      bo_cols[:, oc:oc + 1], ALU.mult, ALU.add, sem=cm.cV)
            p.seq("vector", cm.cV)
            vv = p.op("vector", "scalar_tensor_tensor", cm.xc(oc, tt), tmp,
                      cm.kc[:, 3:4], cm.xc(oc, tt), ALU.mult, ALU.add, sem=cm.cV)
            p.wait("tensor", cm.cV, vv)
    return cm.cV.n


class NS:
    pass


def make_attn_masks(qoff):
    i = np.arange(128)[:, None]
    kk = np.arange(512)[None, :]
    mQ = np.where(kk <= 4 * i + qoff, 0.0, NEG).astype(ml_dtypes.bfloat16)
    ki = np.arange(128)[:, None]
    qq = np.arange(512)[None, :]
    mT = np.stack([np.where(ki <= 4 * qq + qoff - 128 * rb, 0.0, NEG)
                   for rb in range(16)]).astype(ml_dtypes.bfloat16)
    return mQ, mT


def emit_sweep_load(cm, sx, xT_bat, s, tt, after_T):
    p = cm.p
    p.wait("gpsimd", cm.cT, after_T)
    col = s * 1024 + tt * 512
    return p.dma("gpsimd", sx[:].rearrange("p (c t) -> p c t", c=NCH),
                 xT_bat.rearrange("(c p) t -> p c t", p=128)[:, :, col:col + 512],
                 cm.qG)


def emit_attn_l1(cm, xT_bat, wts, bias_srcs, scr_names, pre, ph_unused):
    p, nc = cm.p, cm.nc
    kT_all, v_all, qT_dram, ctxT_dram = (scr_names["kT_all"], scr_names["v_all"],
                                         scr_names["qT"], scr_names["ctxT"])
    bq = nc.alloc_sbuf_tensor("l1a_bq", [128, NCH], F32)
    bk = nc.alloc_sbuf_tensor("l1a_bk", [128, NCH], F32)
    bo = nc.alloc_sbuf_tensor("l1a_bo", [128, NCH], F32)
    p.wait("gpsimd", pre[0], pre[1])
    p.wait("tensor", pre[0], pre[1])
    cm.dbg_bq = bq[:]
    emit_scale_cols(cm, bias_srcs["qb_q"], NCH, bq, fold=1e-4)
    emit_scale_cols(cm, bias_srcs["qb_k"], NCH, bk, fold=1e-4)
    emit_scale_cols(cm, bias_srcs["qb_o"], NCH, bo, fold=1e-4)
    ph_b = Phase(nc)
    brow_v = ph_b.sb("l1a_brv", [1, D], BF16)
    emit_brow(cm, bias_srcs["qb_v"], brow_v, 0.01)
    bias_ready = cm.cV.n
    sx = ph_b.sb("l1a_sx", [128, NCH * TT], F32)
    ktmp = ph_b.sb("l1a_kt", [128, 2 * TT], BF16)
    vtmp = ph_b.sb("l1a_vt", [128, 2 * D], BF16)
    slab = ph_b.sb("l1a_sl", [128, 2 * NCH * 128], BF16)
    vslab = ph_b.sb("l1a_vsl", [128, 2 * NCH * 256], BF16)
    st_done = {}

    def kq_out(dst_dram, col):
        def cb(oc):
            par = oc % 2
            tile = ktmp[:, par * TT:(par + 1) * TT]
            if ("k", par) in st_done:
                p.wait("scalar", cm.qY, st_done[("k", par)])
            return tile
        return cb

    def kq_store(dst_dram, col, oc):
        par = oc % 2
        tile = ktmp[:, par * TT:(par + 1) * TT]
        p.wait("sync", cm.cS, cm.cS.n)
        st_done[("k", par)] = p.dma(
            "sync", dst_dram.rearrange("(c p) t -> p c t", p=128)[:, oc:oc + 1, col:col + 512],
            tile, cm.qY)

    for s in range(4):
        for tt2 in range(2):
            col = s * 1024 + tt2 * 512
            vx = emit_sweep_load(cm, sx, xT_bat, s, tt2, cm.cT.n)
            xr = emit_ln_src_tt(cm, lambda c: sx[:, c * TT:(c + 1) * TT],
                                xwait=((cm.qG, vx), (cm.cV, bias_ready),
                                       pre))
            p.wait("tensor", cm.cV, xr)
            emit_xln_colsum(cm)
            p.wait("gpsimd", pre[0], pre[1])
            for oc in range(NCH):
                t = oc % 2
                sl = slab[:, t * NCH * 128:(t + 1) * NCH * 128]
                if oc >= 2:
                    p.wait("gpsimd", cm.cT, cm.cT.n)
                vq = p.dma("gpsimd", sl,
                           wts["wt_k"].rearrange("(k p) h -> p k h", p=128)[:, :, oc * 128:(oc + 1) * 128],
                           cm.qG)
                bnk = oc % 2
                p.wait("tensor", cm.qG, vq)
                for k in range(NCH):
                    vt = p.op("tensor", "matmul", cm.bank(bnk),
                              sl[:, k * 128:(k + 1) * 128], cm.xln(k),
                              start=(k == 0), stop=False, sem=cm.cT)
                vt = p.op("tensor", "matmul", cm.bank(bnk), cm.negz_rb[0:1, 0:128],
                          cm.mrow[0:1, 0:512], start=False, stop=True, sem=cm.cT)
                p.wait("vector", cm.cT, vt)
                par = oc % 2
                tile = ktmp[:, par * TT:(par + 1) * TT]
                if ("k", par) in st_done:
                    p.wait("vector", cm.qY, st_done[("k", par)])
                vs = p.op("vector", "tensor_scalar", tile, cm.bank(bnk), 0.01,
                          bk[:, oc:oc + 1], ALU.mult, ALU.add, sem=cm.cV)
                p.wait("tensor", cm.cV, vs)
                p.wait("sync", cm.cV, cm.cV.n)
                st_done[("k", par)] = p.dma(
                    "sync", kT_all.rearrange("(c p) t -> p c t", p=128)[:, oc:oc + 1, col:col + 512],
                    tile, cm.qY)
            p.wait("tensor", cm.cV, xr)
            for dw in range(8):
                t = dw % 2
                vsl = vslab[:, t * NCH * 256:(t + 1) * NCH * 256]
                if dw >= 2:
                    p.wait("gpsimd", cm.cT, cm.cT.n)
                vq = p.dma("gpsimd", vsl,
                           wts["wt_v"].rearrange("(k p) h -> p k h", p=128)[:, :, dw * 256:(dw + 1) * 256],
                           cm.qG)
                p.wait("tensor", cm.qG, vq)
                for tc in range(4):
                    bnk = 4 + tc % 2
                    for k in range(NCH):
                        p.op("tensor", "matmul", cm.bank(bnk, width=256),
                             cm.xln(k)[:, tc * 128:(tc + 1) * 128],
                             vsl[:, k * 256:(k + 1) * 256],
                             start=(k == 0), stop=False, sem=cm.cT)
                    p.op("tensor", "matmul", cm.bank(bnk, width=256),
                         cm.mrow[0:1, tc * 128:(tc + 1) * 128],
                         cm.negz_rb[0:1, 0:256], start=False, stop=False, sem=cm.cT)
                    vt = p.op("tensor", "matmul", cm.bank(bnk, width=256),
                              cm.ones_rb[:], brow_v[0:1, dw * 256:(dw + 1) * 256],
                              start=False, stop=True, sem=cm.cT)
                    p.wait("scalar", cm.cT, vt)
                    par = tc % 2
                    vtile = vtmp[:, par * D + dw * 256: par * D + (dw + 1) * 256]
                    if ("v", par, dw) in st_done:
                        p.wait("scalar", cm.qY, st_done[("v", par, dw)])
                    vs = p.op("scalar", "activation", vtile,
                              cm.bank(bnk, width=256), AF.Copy,
                              scale=cm.kc[:, 2:3], sem=cm.cS)
                    p.wait("tensor", cm.cS, vs)
                    p.wait("sync", cm.cS, cm.cS.n)
                    st_done[("v", par, dw)] = p.dma(
                        "sync", v_all[col + tc * 128:col + (tc + 1) * 128,
                                      dw * 256:(dw + 1) * 256], vtile, cm.qY)
    for tt2 in range(2):
        col = tt2 * 512
        xr = emit_ln_tt(cm, tt2)
        p.wait("tensor", cm.cV, xr)
        emit_xln_colsum(cm)
        for oc in range(NCH):
            t = oc % 2
            sl = slab[:, t * NCH * 128:(t + 1) * NCH * 128]
            p.wait("gpsimd", cm.cT, cm.cT.n)
            vq = p.dma("gpsimd", sl,
                       wts["wt_q"].rearrange("(k p) h -> p k h", p=128)[:, :, oc * 128:(oc + 1) * 128],
                       cm.qG)
            bnk = oc % 2
            p.wait("tensor", cm.qG, vq)
            for k in range(NCH):
                vt = p.op("tensor", "matmul", cm.bank(bnk),
                          sl[:, k * 128:(k + 1) * 128], cm.xln(k),
                          start=(k == 0), stop=False, sem=cm.cT)
            vt = p.op("tensor", "matmul", cm.bank(bnk), cm.negz_rb[0:1, 0:128],
                      cm.mrow[0:1, 0:512], start=False, stop=True, sem=cm.cT)
            p.wait("vector", cm.cT, vt)
            par = oc % 2
            tile = ktmp[:, par * TT:(par + 1) * TT]
            if ("k", par) in st_done:
                p.wait("vector", cm.qY, st_done[("k", par)])
            vs = p.op("vector", "tensor_scalar", tile, cm.bank(bnk), 0.01,
                      bq[:, oc:oc + 1], ALU.mult, ALU.add, sem=cm.cV)
            p.wait("tensor", cm.cV, vs)
            p.wait("sync", cm.cV, cm.cV.n)
            st_done[("k", par)] = p.dma(
                "sync", qT_dram.rearrange("(c p) t -> p c t", p=128)[:, oc:oc + 1, col:col + 512],
                tile, cm.qY)
    p.wait("sync", cm.qY, cm.qY.n)
    p.wait("gpsimd", cm.qY, cm.qY.n)
    p.wait("scalar", cm.qY, cm.qY.n)
    ph_b.close()
    ph_c = Phase(nc)
    flb = NS()
    flb.kT = [ph_c.sb("l1a_kTh0", [128, SEQ], BF16), ph_c.sb("l1a_kTh1", [128, SEQ], BF16)]
    flb.vext = [ph_c.sb("l1a_vx0", [128, 32 * 132], BF16),
                ph_c.sb("l1a_vx1", [128, 32 * 132], BF16)]
    flb.qT = [ph_c.sb("l1a_q0", [128, TOK], BF16), ph_c.sb("l1a_q1", [128, TOK], BF16)]
    flb.PT = ph_c.sb("l1a_pt", [128, 2 * 512], BF16)
    flb.ctmp = ph_c.sb("l1a_ctx", [128, TOK], BF16)
    flb.ytile = ph_c.sb("l1a_yt", [128, 128], F32)
    p.op("gpsimd", "memset", flb.vext[0][:], 1.0, sem=cm.cP)
    p.op("gpsimd", "memset", flb.vext[1][:], 1.0, sem=cm.cP)
    p.wait("sync", cm.cP, cm.cP.n)
    p.wait("scalar", cm.cP, cm.cP.n)
    kv_done = cm.qY.n

    def kth_load(h, par):
        p.wait("sync", cm.qY, kv_done)
        return p.dma("sync", flb.kT[par][:], kT_all[h * 128:(h + 1) * 128, :], cm.qY)

    def vext_load(h, par):
        p.wait("scalar", cm.qY, kv_done)
        p.wait("scalar", cm.cT, cm.cT.n)
        return p.dma("scalar",
                     flb.vext[par][:].rearrange("p (b s) -> p b s", b=32)[:, :, 0:128],
                     v_all.rearrange("(b p) d -> p b d", p=128)[:, :, h * 128:(h + 1) * 128],
                     cm.qS)

    emit_flash(cm, 1, 0, NH, kth_load, vext_load, qT_dram, ctxT_dram, flb)
    p.wait("gpsimd", cm.qY, cm.qY.n)
    p.wait("sync", cm.qS, cm.qS.n)
    ph_c.close()
    ph_d = Phase(nc)
    ctx_tile = ph_d.sb("l1a_cx", [128, NCH * TT], BF16)
    oslab = ph_d.sb("l1a_osl", [128, 2 * NCH * 128], BF16)
    emit_attn_oproj(cm, 1, wts["wt_o"], oslab, ctx_tile, ctxT_dram, bo,
                    pre, zerop=True)
    ph_d.close()


def emit_attn_l0(cm, xT_bat, wlat_bf, wd_bf, wo_bf, b_lat, b_d, b_pj, scr,
                 cast_ready, ph_unused):
    p, nc = cm.p, cm.nc
    qT_dram, ctxT_dram = scr["qT"], scr["ctxT"]
    blat = nc.alloc_sbuf_tensor("l0a_bl", [128, 1], F32)
    bd = nc.alloc_sbuf_tensor("l0a_bd", [128, NCH], F32)
    bpj = nc.alloc_sbuf_tensor("l0a_bp", [128, NCH], F32)
    emit_scale_cols(cm, b_lat, 1, blat)
    emit_scale_cols(cm, b_d, NCH, bd)
    emit_scale_cols(cm, b_pj, NCH, bpj)
    bias_ready = cm.cV.n
    ph_c = Phase(nc)
    flb = NS()
    flb.kT = [ph_c.sb("l0a_kT", [128, SEQ], BF16)]
    flb.vext = [ph_c.sb("l0a_vx", [128, 32 * 132], BF16)]
    flb.qT = [ph_c.sb("l0a_q0", [128, TOK], BF16), ph_c.sb("l0a_q1", [128, TOK], BF16)]
    flb.PT = ph_c.sb("l0a_pt", [128, 2 * 512], BF16)
    flb.ctmp = ph_c.sb("l0a_ctx", [128, TOK], BF16)
    flb.ytile = ph_c.sb("l0a_yt", [128, 128], F32)
    flb.maskQ = ph_c.sb("l0a_mq", [128, 512], BF16)
    flb.maskT = ph_c.sb("l0a_mt", [128, 16 * 512], BF16)
    identb = ph_c.sb("l0a_idb", [128, 128], BF16)
    kslab = ph_c.sb("l0a_ksl", [128, NCH * 128], BF16)
    vq = p.dma("gpsimd", flb.maskQ[:], cm.consts_aps["maskQc"], cm.qG)
    vq = p.dma("gpsimd", flb.maskT[:].rearrange("p (b s) -> p b s", b=16),
               cm.consts_aps["maskTc"].rearrange("b p s -> p b s"), cm.qG)
    vq = p.dma("gpsimd", identb[:], cm.consts_aps["identb"], cm.qG)
    p.wait("gpsimd", cm.qG, cast_ready)
    vq = p.dma("gpsimd", kslab[:], wlat_bf.rearrange("(k p) h -> p k h", p=128),
               cm.qG)
    p.wait("vector", cm.qG, vq)
    p.wait("tensor", cm.qG, vq)
    ph_b = Phase(nc)
    sx = ph_b.sb("l0a_sx", [128, NCH * TT], F32)
    qslab = ph_b.sb("l0a_qsl", [128, 2 * NCH * 128], BF16)
    qtmp = ph_b.sb("l0a_qt", [128, 2 * TT], BF16)
    st_done = {}
    for s in range(4):
        for tt2 in range(2):
            vx = emit_sweep_load(cm, sx, xT_bat, s, tt2, cm.cT.n)
            xr = emit_ln_src_tt(cm, lambda c: sx[:, c * TT:(c + 1) * TT],
                                xwait=((cm.qG, vx), (cm.cV, bias_ready),
                                       (cm.qG, cast_ready)))
            p.wait("tensor", cm.cV, xr)
            col = s * 1024 + tt2 * 512
            for k in range(NCH):
                vt = p.op("tensor", "matmul", cm.bank(4),
                          kslab[:, k * 128:(k + 1) * 128], cm.xln(k),
                          start=(k == 0), stop=(k == NCH - 1), sem=cm.cT)
            p.wait("vector", cm.cT, vt)
            vs = p.op("vector", "tensor_scalar_add", flb.kT[0][:, col:col + 512],
                      cm.bank(4), blat[:, 0:1], sem=cm.cV)
            p.wait("tensor", cm.cV, vs)
    for tt2 in range(2):
        col = tt2 * 512
        xr = emit_ln_tt(cm, tt2)
        p.wait("tensor", cm.cV, xr)
        p.wait("gpsimd", cm.qG, cast_ready)
        for oc in range(NCH):
            t = oc % 2
            sl = qslab[:, t * NCH * 128:(t + 1) * NCH * 128]
            p.wait("gpsimd", cm.cT, cm.cT.n)
            vq = p.dma("gpsimd", sl,
                       wd_bf.rearrange("(k p) h -> p k h", p=128)[:, :, oc * 128:(oc + 1) * 128],
                       cm.qG)
            bnk = oc % 2
            p.wait("tensor", cm.qG, vq)
            for k in range(NCH):
                vt = p.op("tensor", "matmul", cm.bank(bnk),
                          sl[:, k * 128:(k + 1) * 128], cm.xln(k),
                          start=(k == 0), stop=(k == NCH - 1), sem=cm.cT)
            p.wait("vector", cm.cT, vt)
            par = oc % 2
            tile = qtmp[:, par * TT:(par + 1) * TT]
            if ("q", par) in st_done:
                p.wait("vector", cm.qY, st_done[("q", par)])
            vs = p.op("vector", "tensor_scalar_add", tile, cm.bank(bnk),
                      bd[:, oc:oc + 1], sem=cm.cV)
            p.wait("tensor", cm.cV, vs)
            p.wait("sync", cm.cV, cm.cV.n)
            st_done[("q", par)] = p.dma(
                "sync", qT_dram.rearrange("(c p) t -> p c t", p=128)[:, oc:oc + 1, col:col + 512],
                tile, cm.qY)
    p.wait("sync", cm.qY, cm.qY.n)
    p.wait("gpsimd", cm.qY, cm.qY.n)
    p.wait("scalar", cm.qY, cm.qY.n)
    ph_b.close()
    p.op("gpsimd", "memset", flb.vext[0][:], 1.0, sem=cm.cP)
    p.wait("tensor", cm.cP, cm.cP.n)
    p.wait("tensor", cm.cS, cm.cS.n)
    for kb in range(32):
        vt = p.op("tensor", "matmul", cm.bank(6, width=128),
                  flb.kT[0][:, kb * 128:(kb + 1) * 128], identb[:],
                  start=True, stop=True, sem=cm.cT)
        p.wait("scalar", cm.cT, vt)
        vs = p.op("scalar", "activation", flb.vext[0][:, kb * 132:kb * 132 + 128],
                  cm.bank(6, width=128), AF.Copy, sem=cm.cS)
        p.wait("tensor", cm.cS, vs)
    emit_flash(cm, 0, 0, NH, None, None, qT_dram, ctxT_dram, flb)
    p.wait("gpsimd", cm.qY, cm.qY.n)
    ph_c.close()
    ph_d = Phase(nc)
    ctx_tile = ph_d.sb("l0a_cx", [128, NCH * TT], BF16)
    oslab = ph_d.sb("l0a_osl", [128, 2 * NCH * 128], BF16)
    emit_attn_oproj(cm, 0, wo_bf, oslab, ctx_tile, ctxT_dram, bpj,
                    (cm.qG, cast_ready), zerop=False)
    ph_d.close()


def build_l0attn_launch():
    nc = bass.Bass("TRN2", target_bir_lowering=False)
    xT = nc.dram_tensor("xT", [D, TOK], F32, kind="ExternalInput").ap()
    xT_bat = nc.dram_tensor("xT_bat", [D, SEQ], F32, kind="ExternalInput").ap()
    wlat = nc.dram_tensor("L0_attn_W_lat", [D, DL], F32, kind="ExternalInput").ap()
    blat = nc.dram_tensor("L0_attn_b_lat", [DL], F32, kind="ExternalInput").ap()
    wd = nc.dram_tensor("L0_attn_W_d", [D, D], F32, kind="ExternalInput").ap()
    bd = nc.dram_tensor("L0_attn_b_d", [D], F32, kind="ExternalInput").ap()
    wpj = nc.dram_tensor("L0_attn_W_proj", [D, D], F32, kind="ExternalInput").ap()
    bpj = nc.dram_tensor("L0_attn_b_proj", [D], F32, kind="ExternalInput").ap()
    consts = decl_consts(nc, attn=True, masks=True)
    out = nc.dram_tensor("xT_out", [D, TOK], F32, kind="ExternalOutput").ap()
    wlat_bf = nc.dram_tensor("wlat_bf", [D, DL], BF16).ap()
    wd_bf = nc.dram_tensor("wd_bf", [D, D], BF16).ap()
    wo_bf = nc.dram_tensor("wo_bf", [D, D], BF16).ap()
    qT = nc.dram_tensor("qT_scr", [D, TOK], BF16).ap()
    ctxT = nc.dram_tensor("ctxT_scr", [D, TOK], BF16).ap()

    p = P(nc)
    cm = Common(p, nc, consts)
    cm.wait_init()
    vx = p.dma("sync", cm.X[:].rearrange("p (c t) -> p c t", c=NCH),
               xT.rearrange("(c p) t -> p c t", p=128), cm.qY)
    cm.xload = vx
    p.dma("gpsimd", wlat_bf, wlat, cm.qG)
    p.dma("gpsimd", wd_bf, wd, cm.qG)
    cast_ready = p.dma("gpsimd", wo_bf, wpj, cm.qG)
    ph = Phase(nc)
    p.wait("tensor", cm.qY, vx)
    p.wait("scalar", cm.qY, vx)
    p.wait("gpsimd", cm.qY, vx)
    emit_attn_l0(cm, xT_bat, wlat_bf, wd_bf, wo_bf, blat, bd, bpj,
                 {"qT": qT, "ctxT": ctxT}, cast_ready, ph)
    ph.close()
    sync_output_store(cm, out)
    p.finalize()
    return nc


def build_l1attn_launch():
    nc = bass.Bass("TRN2", target_bir_lowering=False)
    xT = nc.dram_tensor("xT", [D, TOK], F32, kind="ExternalInput").ap()
    xT_bat = nc.dram_tensor("xT_bat", [D, SEQ], F32, kind="ExternalInput").ap()
    qwq = nc.dram_tensor("L1_attn_qw_q", [D, D], I32, kind="ExternalInput").ap()
    qwk = nc.dram_tensor("L1_attn_qw_k", [D, D], I32, kind="ExternalInput").ap()
    qwv = nc.dram_tensor("L1_attn_qw_v", [D, D], I32, kind="ExternalInput").ap()
    qwo = nc.dram_tensor("L1_attn_qw_o", [D, D], I32, kind="ExternalInput").ap()
    qbq = nc.dram_tensor("L1_attn_qb_q", [D], I32, kind="ExternalInput").ap()
    qbk = nc.dram_tensor("L1_attn_qb_k", [D], I32, kind="ExternalInput").ap()
    qbv = nc.dram_tensor("L1_attn_qb_v", [D], I32, kind="ExternalInput").ap()
    qbo = nc.dram_tensor("L1_attn_qb_o", [D], I32, kind="ExternalInput").ap()
    consts = decl_consts(nc, attn=True, masks=False)
    out = nc.dram_tensor("xT_out", [D, TOK], F32, kind="ExternalOutput").ap()
    scs = {nm: nc.dram_tensor(f"sc_{nm}", [D, D], BF16).ap() for nm in "qkvo"}
    wts = {f"wt_{nm}": nc.dram_tensor(f"wt_{nm}", [D, D], BF16).ap() for nm in "qkvo"}
    kT_all = nc.dram_tensor("kT_all", [D, SEQ], BF16).ap()
    v_all = nc.dram_tensor("v_all", [SEQ, D], BF16).ap()
    qT = nc.dram_tensor("qT_scr", [D, TOK], BF16).ap()
    ctxT = nc.dram_tensor("ctxT_scr", [D, TOK], BF16).ap()

    p = P(nc)
    cm = Common(p, nc, consts)
    cm.wait_init()
    vx = p.dma("sync", cm.X[:].rearrange("p (c t) -> p c t", c=NCH),
               xT.rearrange("(c p) t -> p c t", p=128), cm.qY)
    cvs = emit_cast_weights(cm, [qwq, qwk, qwv, qwo],
                            [scs["q"], scs["k"], scs["v"], scs["o"]])
    wt_ready = emit_transpose_weights(
        cm, [(scs[nm], wts[f"wt_{nm}"], cv) for nm, cv in zip("qkvo", cvs)],
        tallR=D)
    ph = Phase(nc)
    p.wait("tensor", cm.qY, vx)
    p.wait("scalar", cm.qY, vx)
    p.wait("gpsimd", cm.qY, vx)
    emit_attn_l1(cm, xT_bat, wts,
                 {"qb_q": qbq, "qb_k": qbk, "qb_v": qbv, "qb_o": qbo},
                 {"kT_all": kT_all, "v_all": v_all, "qT": qT, "ctxT": ctxT},
                 (cm.qS, wt_ready), ph)
    ph.close()
    sync_output_store(cm, out)
    p.finalize()
    return nc



GROUPS = [[0, 1, 2, 3], [4, 5, 6, 7]]

WEIGHT_KEYS = (
    "L0_attn_W_lat", "L0_attn_b_lat", "L0_attn_W_d", "L0_attn_b_d",
    "L0_attn_W_proj", "L0_attn_b_proj", "L0_ffn_w_fc", "L0_ffn_w_proj",
    "L1_attn_qw_q", "L1_attn_qw_k", "L1_attn_qw_v", "L1_attn_qw_o",
    "L1_attn_qb_q", "L1_attn_qb_k", "L1_attn_qb_v", "L1_attn_qb_o",
    "L1_ffn_gate_w", "L1_ffn_gate_scale", "L1_ffn_up_w", "L1_ffn_up_scale",
    "L1_ffn_down_w", "L1_ffn_down_scale")


def make_fused_masks(qoff):
    i_q = np.arange(128)[:, None]
    i_loc = np.arange(512)[None, :]
    ki = np.arange(128)[:, None]
    qq = np.arange(512)[None, :]
    mq = np.empty((4, 4, 128, 512), ml_dtypes.bfloat16)
    mt = np.empty((4, 4, 128, 512), ml_dtypes.bfloat16)
    for qr in range(4):
        b = 1 if qr > qoff else 0
        for m in range(4):
            mq[qr, m] = np.where(i_loc <= 128 * m + i_q - b, 0.0, NEG)
            mt[qr, m] = np.where(ki <= qq - 128 * m - b, 0.0, NEG)
    return mq, mt


def emit_allgather(cm, in_t, out_t):
    p = cm.p
    s = p.sem(f"ccg{len(p.all_sems)}")
    p.op("gpsimd", "collective_compute", "AllGather", ALU.bypass,
         sem=s, inc=1, replica_groups=GROUPS,
         ins=[in_t.ap().opt()], outs=[out_t.ap().opt()])
    return s


def emit_flash2(cm, causal, nheads, kTh_load, vext_load, qT_dram, ctxT_dram, flb):
    p = cm.p
    nm = cm.cols[:, 8:9]
    rl = cm.cols[:, 9:10]
    for h in range(nheads):
        par = h % 2
        p.wait("sync", cm.cT, cm.cT.n)
        kv_ready = []
        if kTh_load is not None:
            kv_ready.append((cm.qY, kTh_load(h, par)))
        if vext_load is not None:
            kv_ready.append((cm.qS, vext_load(h, par)))
        kT = flb.kT[par] if kTh_load is not None else flb.kT[0]
        vext = flb.vext[par] if vext_load is not None else flb.vext[0]
        vq = p.dma("sync", flb.qT[par][:], qT_dram[h * 128:(h + 1) * 128, :], cm.qY)
        kv_ready.append((cm.qY, vq))
        for s, v in kv_ready:
            p.wait("tensor", s, v)
        for hq in range(2):
            for j in range(4):
                J = 4 * hq + j
                if causal:
                    tiles = []
                    for qr in range(4):
                        for it in range(J // 4):
                            tiles.append((qr * 1024 + it * 512, None))
                    for qr in range(4):
                        tiles.append((qr * 1024 + (J // 4) * 512, (qr, J % 4)))
                else:
                    tiles = [(b * 512, None) for b in range(8)]
                ntl = len(tiles)
                p.wait("tensor", cm.cV, cm.cV.n)
                for bi, (col, mk) in enumerate(tiles):
                    vt = p.op("tensor", "matmul", cm.bank(bi),
                              flb.qT[par][:, hq * 512 + j * 128: hq * 512 + (j + 1) * 128],
                              kT[:, col:col + 512], start=True, stop=True,
                              sem=cm.cT)
                p.wait("vector", cm.cT, vt)
                for bi, (col, mk) in enumerate(tiles):
                    if mk is not None:
                        qr, m = mk
                        sl = (qr * 4 + m) * 512
                        p.op("vector", "tensor_tensor", cm.bank(bi), cm.bank(bi),
                             flb.maskQ[:, sl:sl + 512], ALU.add, sem=cm.cV)
                p.seq("vector", cm.cV)
                p.op("vector", "tensor_reduce", nm, cm.ps[:, 0:ntl * 512], AX.X,
                     ALU.max, negate=True, sem=cm.cV)
                p.seq("vector", cm.cV)
                vv = cm.cV.n
                p.wait("tensor", cm.cV, vv)
                vt = p.op("tensor", "matmul", cm.bank(6, width=128, part=1), nm,
                          cm.ident[:], start=True, stop=True, sem=cm.cT)
                p.wait("scalar", cm.cT, vt)
                vs = p.op("scalar", "activation",
                          cm.mrow[0:1, hq * 512 + j * 128: hq * 512 + (j + 1) * 128],
                          cm.bank(6, width=128, part=1), AF.Copy, sem=cm.cS)
                p.wait("tensor", cm.cS, vs)
                p.wait("vector", cm.cS, vs)
            if causal:
                kbs = []
                for qr in range(4):
                    for it in range(4 * hq + 4):
                        kbs.append((8 * qr + it,
                                    (qr, it - 4 * hq) if it >= 4 * hq else None))
            else:
                kbs = [(kb, None) for kb in range(32)]
            pt_used = {}
            first_kb = True
            nkbs = len(kbs)
            for ki, (kb, mk) in enumerate(kbs):
                tpar = ki % 2
                bA = tpar
                pt = flb.PT[:, tpar * 512:(tpar + 1) * 512]
                p.wait("tensor", cm.cS, cm.cS.n)
                p.op("tensor", "matmul", cm.bank(bA),
                     kT[:, kb * 128:(kb + 1) * 128],
                     flb.qT[par][:, hq * 512:(hq + 1) * 512], start=True, stop=False,
                     sem=cm.cT)
                vt = p.op("tensor", "matmul", cm.bank(bA), cm.ones_rb[:],
                          cm.mrow[0:1, hq * 512:(hq + 1) * 512], start=False,
                          stop=True, sem=cm.cT)
                if mk is not None:
                    qr, r = mk
                    sl = (qr * 4 + r) * 512
                    p.wait("vector", cm.cT, vt)
                    vv = p.op("vector", "tensor_tensor", cm.bank(bA), cm.bank(bA),
                              flb.maskT[:, sl:sl + 512], ALU.add, sem=cm.cV)
                    p.wait("scalar", cm.cV, vv)
                p.wait("scalar", cm.cT, vt)
                if tpar in pt_used:
                    p.wait("scalar", cm.cT, pt_used[tpar])
                vs = p.op("scalar", "activation", pt, cm.bank(bA), AF.Exp,
                          scale=cm.kc[:, 1:2], sem=cm.cS)
                p.wait("tensor", cm.cS, vs)
                for j in range(4):
                    vt = p.op("tensor", "matmul", cm.bank(2 + j, width=129),
                              pt[:, j * 128:(j + 1) * 128],
                              vext[:, kb * 132:kb * 132 + 129],
                              start=first_kb, stop=(ki == nkbs - 1), sem=cm.cT)
                pt_used[tpar] = vt
                first_kb = False
            for j in range(4):
                pv = cm.bank(2 + j, width=129)
                p.wait("vector", cm.cT, cm.cT.n)
                p.op("vector", "reciprocal", rl, pv[:, 128:129], sem=cm.cV)
                p.seq("vector", cm.cV)
                vv = p.op("vector", "tensor_scalar_mul", flb.ytile[:], pv[:, 0:128],
                          rl[:, 0:1], sem=cm.cV)
                p.wait("tensor", cm.cV, vv)
                vt = p.op("tensor", "matmul", cm.bank(6, width=128), flb.ytile[:],
                          cm.ident[:], start=True, stop=True, sem=cm.cT)
                p.wait("scalar", cm.cT, vt)
                vs = p.op("scalar", "activation",
                          flb.ctmp[:, hq * 512 + j * 128: hq * 512 + (j + 1) * 128],
                          cm.bank(6, width=128), AF.Copy, sem=cm.cS)
                p.wait("tensor", cm.cS, vs)
                p.wait("vector", cm.cS, vs)
        p.wait("sync", cm.cS, cm.cS.n)
        vo = p.dma("sync", ctxT_dram[h * 128:(h + 1) * 128, :], flb.ctmp[:], cm.qY)
        p.wait("scalar", cm.qY, vo)
    return cm.qY.n


def emit_drain(cm, engs=("sync", "gpsimd", "scalar", "vector", "tensor")):
    p = cm.p
    for eng in engs:
        for s in (cm.qG, cm.qS, cm.qY, cm.cT, cm.cV, cm.cS, cm.cP):
            p.wait(eng, s, s.n)


def emit_l0ffn_block(cm, wt_fc, wt_pj, wt_ready, xload):
    nc = cm.nc
    cm.n0f = getattr(cm, "n0f", 0) + 1
    ph = Phase(nc)
    hbuf = ph.sb(f"f_l0_h{cm.n0f}", [128, NH0 * TT], BF16)
    fslab = ph.sb(f"f_l0_fs{cm.n0f}", [128, 2 * NCH * 128], BF16)
    pslab = ph.sb(f"f_l0_ps{cm.n0f}", [128, 2 * NH0 * 128], BF16)
    for tt in range(NTT):
        xr = emit_ln_tt(cm, tt, xwait=((cm.qY, xload), (cm.cV, cm.cV.n)))
        emit_ffn_l0_tt(cm, tt, wt_fc, wt_pj, wt_ready, xr, hbuf, fslab, pslab)
    emit_drain(cm)
    ph.close()


def emit_l1ffn_block(cm, wt_g, wt_u, wt_d, gsc, usc, dsc, wt_ready, xload):
    nc = cm.nc
    cm.n1f = getattr(cm, "n1f", 0) + 1
    ph = Phase(nc)
    hbuf = ph.sb(f"f_l1_h{cm.n1f}", [128, NI1 * TT], BF16)
    gslab = ph.sb(f"f_l1_gs{cm.n1f}", [128, 2 * NCH * 128], BF16)
    uslab = ph.sb(f"f_l1_us{cm.n1f}", [128, 2 * NCH * 128], BF16)
    dslab = ph.sb(f"f_l1_ds{cm.n1f}", [128, 2 * 16 * 128], BF16)
    for tt in range(NTT):
        xr = emit_ln_tt(cm, tt, xwait=((cm.qY, xload), (cm.cV, cm.cV.n)))
        emit_ffn_l1_tt(cm, tt, wt_g, wt_u, wt_d, gsc, usc, dsc, wt_ready, xr,
                       hbuf, gslab, uslab, dslab)
    emit_drain(cm)
    ph.close()


def emit_attn_l0_fused(cm, wlat_bf, wd_bf, wo_bf, b_lat, b_d, b_pj, scr,
                       cast_ready, xload):
    p, nc = cm.p, cm.nc
    qT_dram, ctxT_dram, cc_in, cc_out = (scr["qT"], scr["ctxT"], scr["cc_in"],
                                         scr["cc_out"])
    blat = nc.alloc_sbuf_tensor("f0a_bl", [128, 1], F32)
    bd = nc.alloc_sbuf_tensor("f0a_bd", [128, NCH], F32)
    bpj = nc.alloc_sbuf_tensor("f0a_bp", [128, NCH], F32)
    emit_scale_cols(cm, b_lat, 1, blat)
    emit_scale_cols(cm, b_d, NCH, bd)
    emit_scale_cols(cm, b_pj, NCH, bpj)
    ph_c = Phase(nc)
    flb = NS()
    flb.kT = [ph_c.sb("f0a_kT", [128, SEQ], BF16)]
    flb.vext = [ph_c.sb("f0a_vx", [128, 32 * 132], BF16)]
    flb.qT = [ph_c.sb("f0a_q0", [128, TOK], BF16), ph_c.sb("f0a_q1", [128, TOK], BF16)]
    flb.PT = ph_c.sb("f0a_pt", [128, 2 * 512], BF16)
    flb.ctmp = ph_c.sb("f0a_ctx", [128, TOK], BF16)
    flb.ytile = ph_c.sb("f0a_yt", [128, 128], F32)
    flb.maskQ = ph_c.sb("f0a_mq", [128, 16 * 512], BF16)
    flb.maskT = ph_c.sb("f0a_mt", [128, 16 * 512], BF16)
    identb = ph_c.sb("f0a_idb", [128, 128], BF16)
    kslab = ph_c.sb("f0a_ksl", [128, NCH * 128], BF16)
    ktile = ph_c.sb("f0a_ktile", [128, TT], BF16)
    p.dma("gpsimd", flb.maskQ[:].rearrange("p (q m s) -> p q m s", q=4, m=4),
          cm.consts_aps["mq"].rearrange("q m p s -> p q m s"), cm.qG)
    p.dma("gpsimd", flb.maskT[:].rearrange("p (q m s) -> p q m s", q=4, m=4),
          cm.consts_aps["mt"].rearrange("q m p s -> p q m s"), cm.qG)
    vq = p.dma("gpsimd", identb[:], cm.consts_aps["identb"], cm.qG)
    p.wait("gpsimd", cm.qG, cast_ready)
    vq = p.dma("gpsimd", kslab[:], wlat_bf.rearrange("(k p) h -> p k h", p=128),
               cm.qG)
    p.wait("vector", cm.qG, vq)
    p.wait("tensor", cm.qG, vq)
    mask_ready = vq
    ph_b = Phase(nc)
    qslab = ph_b.sb("f0a_qsl", [128, 2 * NCH * 128], BF16)
    qtmp = ph_b.sb("f0a_qt", [128, 2 * TT], BF16)
    st_done = {}
    kst = None
    for tt in range(NTT):
        xr = emit_ln_tt(cm, tt, xwait=((cm.qY, xload), (cm.cV, cm.cV.n)))
        p.wait("tensor", cm.cV, xr)
        for k in range(NCH):
            vt = p.op("tensor", "matmul", cm.bank(4), kslab[:, k * 128:(k + 1) * 128],
                      cm.xln(k), start=(k == 0), stop=(k == NCH - 1), sem=cm.cT)
        p.wait("vector", cm.cT, vt)
        if kst is not None:
            p.wait("vector", cm.qY, kst)
        vs = p.op("vector", "tensor_scalar_add", ktile[:], cm.bank(4),
                  blat[:, 0:1], sem=cm.cV)
        p.wait("tensor", cm.cV, vs)
        p.wait("sync", cm.cV, vs)
        kst = p.dma("sync", cc_in[:, tt * TT:(tt + 1) * TT], ktile[:], cm.qY)
        for oc in range(NCH):
            t = oc % 2
            sl = qslab[:, t * NCH * 128:(t + 1) * NCH * 128]
            p.wait("gpsimd", cm.cT, cm.cT.n)
            vq = p.dma("gpsimd", sl,
                       wd_bf.rearrange("(k p) h -> p k h", p=128)[:, :, oc * 128:(oc + 1) * 128],
                       cm.qG)
            bnk = oc % 2
            p.wait("tensor", cm.qG, vq)
            for k in range(NCH):
                vt = p.op("tensor", "matmul", cm.bank(bnk),
                          sl[:, k * 128:(k + 1) * 128], cm.xln(k),
                          start=(k == 0), stop=(k == NCH - 1), sem=cm.cT)
            p.wait("vector", cm.cT, vt)
            par = oc % 2
            tile = qtmp[:, par * TT:(par + 1) * TT]
            if ("q", par) in st_done:
                p.wait("vector", cm.qY, st_done[("q", par)])
            vs = p.op("vector", "tensor_scalar_add", tile, cm.bank(bnk),
                      bd[:, oc:oc + 1], sem=cm.cV)
            p.wait("tensor", cm.cV, vs)
            p.wait("sync", cm.cV, cm.cV.n)
            st_done[("q", par)] = p.dma(
                "sync", qT_dram.rearrange("(c p) t -> p c t", p=128)[:, oc:oc + 1, tt * TT:(tt + 1) * TT],
                tile, cm.qY)
    p.wait("gpsimd", cm.qY, cm.qY.n)
    p.wait("scalar", cm.qY, cm.qY.n)
    p.wait("sync", cm.qY, cm.qY.n)
    ph_b.close()
    ccs = emit_allgather(cm, scr["cc_in_t"], scr["cc_out_t"])
    p.wait("sync", ccs, 1)
    vk = p.dma("sync", flb.kT[0][:].rearrange("p (q i) -> p q i", q=4),
               cc_out.rearrange("(q d) i -> d q i", q=4), cm.qY)
    p.wait("tensor", cm.qY, vk)
    p.wait("gpsimd", cm.qY, vk)
    p.op("gpsimd", "memset", flb.vext[0][:], 1.0, sem=cm.cP)
    p.wait("tensor", cm.cP, cm.cP.n)
    p.wait("tensor", cm.cS, cm.cS.n)
    p.wait("tensor", cm.qG, mask_ready)
    p.wait("vector", cm.qG, mask_ready)
    for kb in range(32):
        vt = p.op("tensor", "matmul", cm.bank(6, width=128),
                  flb.kT[0][:, kb * 128:(kb + 1) * 128], identb[:],
                  start=True, stop=True, sem=cm.cT)
        p.wait("scalar", cm.cT, vt)
        vs = p.op("scalar", "activation", flb.vext[0][:, kb * 132:kb * 132 + 128],
                  cm.bank(6, width=128), AF.Copy, sem=cm.cS)
        p.wait("tensor", cm.cS, vs)
    emit_flash2(cm, True, NH, None, None, qT_dram, ctxT_dram, flb)
    p.wait("gpsimd", cm.qY, cm.qY.n)
    emit_drain(cm)
    ph_c.close()
    ph_d = Phase(nc)
    ctx_tile = ph_d.sb("f0a_cx", [128, NCH * TT], BF16)
    oslab = ph_d.sb("f0a_osl", [128, 2 * NCH * 128], BF16)
    emit_attn_oproj(cm, 0, wo_bf, oslab, ctx_tile, ctxT_dram, bpj,
                    (cm.qG, cast_ready), zerop=False)
    emit_drain(cm)
    ph_d.close()


def emit_attn_l1_fused(cm, wts, bias_srcs, scr, wt_ready, xload):
    p, nc = cm.p, cm.nc
    qT_dram, ctxT_dram = scr["qT"], scr["ctxT"]
    cck_in, cck_out = scr["cck_in"], scr["cck_out"]
    ccv_in, ccv_out = scr["ccv_in"], scr["ccv_out"]
    bq = nc.alloc_sbuf_tensor("f1a_bq", [128, NCH], F32)
    bk = nc.alloc_sbuf_tensor("f1a_bk", [128, NCH], F32)
    bo = nc.alloc_sbuf_tensor("f1a_bo", [128, NCH], F32)
    p.wait("gpsimd", cm.qS, wt_ready)
    p.wait("tensor", cm.qS, wt_ready)
    emit_scale_cols(cm, bias_srcs["qb_q"], NCH, bq, fold=1e-4)
    emit_scale_cols(cm, bias_srcs["qb_k"], NCH, bk, fold=1e-4)
    emit_scale_cols(cm, bias_srcs["qb_o"], NCH, bo, fold=1e-4)
    ph_b = Phase(nc)
    brow_v = ph_b.sb("f1a_brv", [1, D], BF16)
    emit_brow(cm, bias_srcs["qb_v"], brow_v, 0.01)
    ktmp = ph_b.sb("f1a_kt", [128, 2 * TT], BF16)
    vtmp = ph_b.sb("f1a_vt", [128, 2 * D], BF16)
    slab = ph_b.sb("f1a_sl", [128, 2 * NCH * 128], BF16)
    vslab = ph_b.sb("f1a_vsl", [128, 2 * NCH * 256], BF16)
    st_done = {}
    for tt in range(NTT):
        col = tt * TT
        xr = emit_ln_tt(cm, tt, xwait=((cm.qY, xload), (cm.cV, cm.cV.n)))
        p.wait("tensor", cm.cV, xr)
        emit_xln_colsum(cm)
        for oc in range(NCH):
            t = oc % 2
            sl = slab[:, t * NCH * 128:(t + 1) * NCH * 128]
            p.wait("gpsimd", cm.cT, cm.cT.n)
            vq = p.dma("gpsimd", sl,
                       wts["wt_k"].rearrange("(k p) h -> p k h", p=128)[:, :, oc * 128:(oc + 1) * 128],
                       cm.qG)
            bnk = oc % 2
            p.wait("tensor", cm.qG, vq)
            for k in range(NCH):
                vt = p.op("tensor", "matmul", cm.bank(bnk),
                          sl[:, k * 128:(k + 1) * 128], cm.xln(k),
                          start=(k == 0), stop=False, sem=cm.cT)
            vt = p.op("tensor", "matmul", cm.bank(bnk), cm.negz_rb[0:1, 0:128],
                      cm.mrow[0:1, 0:512], start=False, stop=True, sem=cm.cT)
            p.wait("vector", cm.cT, vt)
            par = oc % 2
            tile = ktmp[:, par * TT:(par + 1) * TT]
            if ("k", par) in st_done:
                p.wait("vector", cm.qY, st_done[("k", par)])
            vs = p.op("vector", "tensor_scalar", tile, cm.bank(bnk), 0.01,
                      bk[:, oc:oc + 1], ALU.mult, ALU.add, sem=cm.cV)
            p.wait("tensor", cm.cV, vs)
            p.wait("sync", cm.cV, cm.cV.n)
            st_done[("k", par)] = p.dma(
                "sync", cck_in.rearrange("(c p) t -> p c t", p=128)[:, oc:oc + 1, col:col + 512],
                tile, cm.qY)
        p.wait("tensor", cm.cV, xr)
        for dw in range(8):
            t = dw % 2
            vsl = vslab[:, t * NCH * 256:(t + 1) * NCH * 256]
            p.wait("gpsimd", cm.cT, cm.cT.n)
            vq = p.dma("gpsimd", vsl,
                       wts["wt_v"].rearrange("(k p) h -> p k h", p=128)[:, :, dw * 256:(dw + 1) * 256],
                       cm.qG)
            p.wait("tensor", cm.qG, vq)
            for tc in range(4):
                bnk = 4 + tc % 2
                for k in range(NCH):
                    p.op("tensor", "matmul", cm.bank(bnk, width=256),
                         cm.xln(k)[:, tc * 128:(tc + 1) * 128],
                         vsl[:, k * 256:(k + 1) * 256],
                         start=(k == 0), stop=False, sem=cm.cT)
                p.op("tensor", "matmul", cm.bank(bnk, width=256),
                     cm.mrow[0:1, tc * 128:(tc + 1) * 128],
                     cm.negz_rb[0:1, 0:256], start=False, stop=False, sem=cm.cT)
                vt = p.op("tensor", "matmul", cm.bank(bnk, width=256),
                          cm.ones_rb[:], brow_v[0:1, dw * 256:(dw + 1) * 256],
                          start=False, stop=True, sem=cm.cT)
                p.wait("scalar", cm.cT, vt)
                par = tc % 2
                vtile = vtmp[:, par * D + dw * 256: par * D + (dw + 1) * 256]
                if ("v", par, dw) in st_done:
                    p.wait("scalar", cm.qY, st_done[("v", par, dw)])
                vs = p.op("scalar", "activation", vtile,
                          cm.bank(bnk, width=256), AF.Copy,
                          scale=cm.kc[:, 2:3], sem=cm.cS)
                p.wait("tensor", cm.cS, vs)
                p.wait("sync", cm.cS, cm.cS.n)
                st_done[("v", par, dw)] = p.dma(
                    "sync", ccv_in[col + tc * 128:col + (tc + 1) * 128,
                                   dw * 256:(dw + 1) * 256], vtile, cm.qY)
        p.wait("tensor", cm.cV, xr)
        for oc in range(NCH):
            t = oc % 2
            sl = slab[:, t * NCH * 128:(t + 1) * NCH * 128]
            p.wait("gpsimd", cm.cT, cm.cT.n)
            vq = p.dma("gpsimd", sl,
                       wts["wt_q"].rearrange("(k p) h -> p k h", p=128)[:, :, oc * 128:(oc + 1) * 128],
                       cm.qG)
            bnk = oc % 2
            p.wait("tensor", cm.qG, vq)
            for k in range(NCH):
                vt = p.op("tensor", "matmul", cm.bank(bnk),
                          sl[:, k * 128:(k + 1) * 128], cm.xln(k),
                          start=(k == 0), stop=False, sem=cm.cT)
            vt = p.op("tensor", "matmul", cm.bank(bnk), cm.negz_rb[0:1, 0:128],
                      cm.mrow[0:1, 0:512], start=False, stop=True, sem=cm.cT)
            p.wait("vector", cm.cT, vt)
            par = oc % 2
            tile = ktmp[:, par * TT:(par + 1) * TT]
            if ("k", par) in st_done:
                p.wait("vector", cm.qY, st_done[("k", par)])
            vs = p.op("vector", "tensor_scalar", tile, cm.bank(bnk), 0.01,
                      bq[:, oc:oc + 1], ALU.mult, ALU.add, sem=cm.cV)
            p.wait("tensor", cm.cV, vs)
            p.wait("sync", cm.cV, cm.cV.n)
            st_done[("k", par)] = p.dma(
                "sync", qT_dram.rearrange("(c p) t -> p c t", p=128)[:, oc:oc + 1, col:col + 512],
                tile, cm.qY)
    p.wait("sync", cm.qY, cm.qY.n)
    p.wait("gpsimd", cm.qY, cm.qY.n)
    p.wait("scalar", cm.qY, cm.qY.n)
    ph_b.close()
    ccs_k = emit_allgather(cm, scr["cck_in_t"], scr["cck_out_t"])
    ccs_v = emit_allgather(cm, scr["ccv_in_t"], scr["ccv_out_t"])
    ph_c = Phase(nc)
    flb = NS()
    flb.kT = [ph_c.sb("f1a_kTh0", [128, SEQ], BF16), ph_c.sb("f1a_kTh1", [128, SEQ], BF16)]
    flb.vext = [ph_c.sb("f1a_vx0", [128, 32 * 132], BF16),
                ph_c.sb("f1a_vx1", [128, 32 * 132], BF16)]
    flb.qT = [ph_c.sb("f1a_q0", [128, TOK], BF16), ph_c.sb("f1a_q1", [128, TOK], BF16)]
    flb.PT = ph_c.sb("f1a_pt", [128, 2 * 512], BF16)
    flb.ctmp = ph_c.sb("f1a_ctx", [128, TOK], BF16)
    flb.ytile = ph_c.sb("f1a_yt", [128, 128], F32)
    flb.maskQ = None
    flb.maskT = None
    p.op("gpsimd", "memset", flb.vext[0][:], 1.0, sem=cm.cP)
    p.op("gpsimd", "memset", flb.vext[1][:], 1.0, sem=cm.cP)
    p.wait("sync", cm.cP, cm.cP.n)
    p.wait("scalar", cm.cP, cm.cP.n)
    kview = cck_out.rearrange("(q d) i -> d q i", q=4)

    def kth_load(h, par):
        p.wait("sync", ccs_k, 1)
        return p.dma("sync", flb.kT[par][:].rearrange("p (q i) -> p q i", q=4),
                     kview[h * 128:(h + 1) * 128], cm.qY)

    def vext_load(h, par):
        p.wait("scalar", ccs_v, 1)
        p.wait("scalar", cm.cT, cm.cT.n)
        return p.dma("scalar",
                     flb.vext[par][:].rearrange("p (b s) -> p b s", b=32)[:, :, 0:128],
                     ccv_out.rearrange("(b p) d -> p b d", p=128)[:, :, h * 128:(h + 1) * 128],
                     cm.qS)

    emit_flash2(cm, False, NH, kth_load, vext_load, qT_dram, ctxT_dram, flb)
    p.wait("gpsimd", cm.qY, cm.qY.n)
    p.wait("sync", cm.qS, cm.qS.n)
    emit_drain(cm)
    ph_c.close()
    ph_d = Phase(nc)
    ctx_tile = ph_d.sb("f1a_cx", [128, NCH * TT], BF16)
    oslab = ph_d.sb("f1a_osl", [128, 2 * NCH * 128], BF16)
    emit_attn_oproj(cm, 1, wts["wt_o"], oslab, ctx_tile, ctxT_dram, bo,
                    (cm.qS, wt_ready), zerop=True)
    emit_drain(cm)
    ph_d.close()


def build_fused(nsub=6, dbg=False, prep="all"):
    nc = bass.Bass("TRN2", target_bir_lowering=False,
                   dynamic_dma_scratch_size=16384)
    xT = nc.dram_tensor("xT", [D, TOK], F32, kind="ExternalInput").ap()
    w_fc = nc.dram_tensor("L0_ffn_w_fc", [H0, D], F32, kind="ExternalInput").ap()
    w_pj = nc.dram_tensor("L0_ffn_w_proj", [D, H0], F32, kind="ExternalInput").ap()
    wlat = nc.dram_tensor("L0_attn_W_lat", [D, DL], F32, kind="ExternalInput").ap()
    blat = nc.dram_tensor("L0_attn_b_lat", [DL], F32, kind="ExternalInput").ap()
    wd = nc.dram_tensor("L0_attn_W_d", [D, D], F32, kind="ExternalInput").ap()
    bd = nc.dram_tensor("L0_attn_b_d", [D], F32, kind="ExternalInput").ap()
    wpj = nc.dram_tensor("L0_attn_W_proj", [D, D], F32, kind="ExternalInput").ap()
    bpj = nc.dram_tensor("L0_attn_b_proj", [D], F32, kind="ExternalInput").ap()
    qwq = nc.dram_tensor("L1_attn_qw_q", [D, D], I32, kind="ExternalInput").ap()
    qwk = nc.dram_tensor("L1_attn_qw_k", [D, D], I32, kind="ExternalInput").ap()
    qwv = nc.dram_tensor("L1_attn_qw_v", [D, D], I32, kind="ExternalInput").ap()
    qwo = nc.dram_tensor("L1_attn_qw_o", [D, D], I32, kind="ExternalInput").ap()
    qbq = nc.dram_tensor("L1_attn_qb_q", [D], I32, kind="ExternalInput").ap()
    qbk = nc.dram_tensor("L1_attn_qb_k", [D], I32, kind="ExternalInput").ap()
    qbv = nc.dram_tensor("L1_attn_qb_v", [D], I32, kind="ExternalInput").ap()
    qbo = nc.dram_tensor("L1_attn_qb_o", [D], I32, kind="ExternalInput").ap()
    gw = nc.dram_tensor("L1_ffn_gate_w", [I1, D], I32, kind="ExternalInput").ap()
    gs = nc.dram_tensor("L1_ffn_gate_scale", [I1], F32, kind="ExternalInput").ap()
    uw = nc.dram_tensor("L1_ffn_up_w", [I1, D], I32, kind="ExternalInput").ap()
    us = nc.dram_tensor("L1_ffn_up_scale", [I1], F32, kind="ExternalInput").ap()
    dw_ = nc.dram_tensor("L1_ffn_down_w", [D, I1], I32, kind="ExternalInput").ap()
    ds = nc.dram_tensor("L1_ffn_down_scale", [D], F32, kind="ExternalInput").ap()
    consts = decl_consts(nc, attn=True)
    consts["mq"] = nc.dram_tensor("c_mq", [4, 4, 128, 512], BF16,
                                  kind="ExternalInput").ap()
    consts["mt"] = nc.dram_tensor("c_mt", [4, 4, 128, 512], BF16,
                                  kind="ExternalInput").ap()
    out = nc.dram_tensor("xT_out", [D, TOK], F32, kind="ExternalOutput").ap()
    sg_fc = nc.dram_tensor("sg_fc", [H0, D], BF16).ap()
    sg_pj = nc.dram_tensor("sg_pj", [D, H0], BF16).ap()
    wt_fc = nc.dram_tensor("wt_fc", [D, H0], BF16).ap()
    wt_pj = nc.dram_tensor("wt_pj", [H0, D], BF16).ap()
    wlat_bf = nc.dram_tensor("wlat_bf", [D, DL], BF16).ap()
    wd_bf = nc.dram_tensor("wd_bf", [D, D], BF16).ap()
    wo_bf = nc.dram_tensor("wo_bf", [D, D], BF16).ap()
    scs1 = {nm: nc.dram_tensor(f"sc1_{nm}", [D, D], BF16).ap() for nm in "qkvo"}
    wts1 = {f"wt_{nm}": nc.dram_tensor(f"wt1_{nm}", [D, D], BF16).ap() for nm in "qkvo"}
    sc_g = nc.dram_tensor("sc_g", [I1, D], BF16).ap()
    sc_u = nc.dram_tensor("sc_u", [I1, D], BF16).ap()
    sc_d = nc.dram_tensor("sc_d", [D, I1], BF16).ap()
    wt_g = nc.dram_tensor("wt_g", [D, I1], BF16).ap()
    wt_u = nc.dram_tensor("wt_u", [D, I1], BF16).ap()
    wt_d = nc.dram_tensor("wt_d", [I1, D], BF16).ap()
    qT0 = nc.dram_tensor("qT0_scr", [D, TOK], BF16).ap()
    ctxT0 = nc.dram_tensor("ctxT0_scr", [D, TOK], BF16).ap()
    qT1 = nc.dram_tensor("qT1_scr", [D, TOK], BF16).ap()
    ctxT1 = nc.dram_tensor("ctxT1_scr", [D, TOK], BF16).ap()
    cc_in_t = nc.dram_tensor("cc0_in", [DL, TOK], BF16)
    cc_out_t = nc.dram_tensor("cc0_out", [4 * DL, TOK], BF16)
    cck_in_t = nc.dram_tensor("cck_in", [D, TOK], BF16)
    cck_out_t = nc.dram_tensor("cck_out", [4 * D, TOK], BF16)
    ccv_in_t = nc.dram_tensor("ccv_in", [TOK, D], BF16)
    ccv_out_t = nc.dram_tensor("ccv_out", [4 * TOK, D], BF16)

    p = P(nc)
    cm = Common(p, nc, consts)
    cm.wait_init()
    vx = p.dma("sync", cm.X[:].rearrange("p (c t) -> p c t", c=NCH),
               xT.rearrange("(c p) t -> p c t", p=128), cm.qY)
    tern_ready = emit_tern_prep(cm, w_fc, w_pj, sg_fc, sg_pj, wt_fc, wt_pj)
    l0a_cast = wt1_ready = wtf_ready = None
    gsc = usc = dsc = None
    if prep in ("all", "l0a"):
        p.dma("gpsimd", wlat_bf, wlat, cm.qG)
        p.dma("gpsimd", wd_bf, wd, cm.qG)
        l0a_cast = p.dma("gpsimd", wo_bf, wpj, cm.qG)
    if prep in ("all", "l1a"):
        cvs1 = emit_cast_weights(cm, [qwq, qwk, qwv, qwo],
                                 [scs1["q"], scs1["k"], scs1["v"], scs1["o"]])
        wt1_ready = emit_transpose_weights(
            cm, [(scs1[nm], wts1[f"wt_{nm}"], cv) for nm, cv in zip("qkvo", cvs1)],
            tallR=D)
    if prep in ("all", "l1f"):
        cvsf = emit_cast_weights(cm, [gw, uw, dw_], [sc_g, sc_u, sc_d])
        wtf_ready = emit_transpose_weights(
            cm, list(zip([sc_g, sc_u, sc_d], [wt_g, wt_u, wt_d], cvsf)), tallR=I1)
    if prep in ("all", "scales"):
        gsc = nc.alloc_sbuf_tensor("f_gsc", [128, NI1], F32)
        usc = nc.alloc_sbuf_tensor("f_usc", [128, NI1], F32)
        dsc = nc.alloc_sbuf_tensor("f_dsc", [128, NCH], F32)
        emit_scale_cols(cm, gs, NI1, gsc)
        emit_scale_cols(cm, us, NI1, usc)
        emit_scale_cols(cm, ds, NCH, dsc, fold=0.5)
    emit_drain(cm)
    subs = [
        lambda: emit_l0ffn_block(cm, wt_fc, wt_pj, tern_ready, vx),
        lambda: emit_attn_l0_fused(cm, wlat_bf, wd_bf, wo_bf, blat, bd, bpj,
                                   {"qT": qT0, "ctxT": ctxT0,
                                    "cc_in": cc_in_t.ap(), "cc_in_t": cc_in_t,
                                    "cc_out": cc_out_t.ap(),
                                    "cc_out_t": cc_out_t}, l0a_cast, vx),
        lambda: emit_l0ffn_block(cm, wt_fc, wt_pj, tern_ready, vx),
        lambda: emit_l1ffn_block(cm, wt_g, wt_u, wt_d, gsc, usc, dsc, wtf_ready, vx),
        lambda: emit_attn_l1_fused(cm, wts1,
                                   {"qb_q": qbq, "qb_k": qbk, "qb_v": qbv,
                                    "qb_o": qbo},
                                   {"qT": qT1, "ctxT": ctxT1,
                                    "cck_in": cck_in_t.ap(), "cck_in_t": cck_in_t,
                                    "cck_out": cck_out_t.ap(),
                                    "cck_out_t": cck_out_t,
                                    "ccv_in": ccv_in_t.ap(), "ccv_in_t": ccv_in_t,
                                    "ccv_out": ccv_out_t.ap(),
                                    "ccv_out_t": ccv_out_t},
                                   wt1_ready, vx),
        lambda: emit_l1ffn_block(cm, wt_g, wt_u, wt_d, gsc, usc, dsc, wtf_ready, vx),
    ]
    for i in range(nsub):
        subs[i]()
    if dbg:
        dq = nc.dram_tensor("dbg_q", [D, TOK], BF16, kind="ExternalOutput").ap()
        dk = nc.dram_tensor("dbg_k", [4 * DL, TOK], BF16, kind="ExternalOutput").ap()
        dc = nc.dram_tensor("dbg_c", [D, TOK], BF16, kind="ExternalOutput").ap()
        emit_drain(cm)
        p.dma("sync", dq, qT0, cm.qY)
        p.dma("sync", dk, cc_out_t.ap(), cm.qY)
        p.dma("sync", dc, ctxT0, cm.qY)
        p.wait("sync", cm.qY, cm.qY.n)
    sync_output_store(cm, out)
    p.finalize()
    return nc


_NC_CACHE = {}


def _get_nc(name):
    if name not in _NC_CACHE:
        _NC_CACHE[name] = {
            "l0ffn": build_l0ffn_launch,
            "l1ffn": build_l1ffn_launch,
            "l0attn": build_l0attn_launch,
            "l1attn": build_l1attn_launch,
            "fused": build_fused,
        }[name]()
    return _NC_CACHE[name]


def _shard(xT_full, c):
    b, q = c // 4, c % 4
    return np.ascontiguousarray(xT_full[:, b * SEQ + q:(b + 1) * SEQ:4])


def _gather(slices):
    full = np.empty((D, 2 * SEQ), np.float32)
    for c in range(8):
        b, q = c // 4, c % 4
        full[:, b * SEQ + q:(b + 1) * SEQ:4] = slices[c]
    return full


def kernel(**inputs):
    consts = make_consts()
    x = np.asarray(inputs["L0_attn_x"], dtype=np.float32)
    xT = np.ascontiguousarray(x.reshape(2 * SEQ, D).T)
    nc = _get_nc("fused")
    in_maps = []
    for c in range(8):
        mq, mt = make_fused_masks(c % 4)
        m = {"xT": _shard(xT, c), "c_ident": consts["c_ident"],
             "c_identb": consts["c_identb"], "c_mq": mq, "c_mt": mt}
        for k in WEIGHT_KEYS:
            m[k] = inputs[k]
        in_maps.append(m)
    res = run_bass_kernel_spmd(nc, in_maps, core_ids=list(range(8)))
    out = _gather([r["xT_out"] for r in res.results])
    return np.ascontiguousarray(out.T).reshape(2, SEQ, D)
